# version 03 (final)
# baseline (speedup 1.0000x reference)
import numpy as np
import ml_dtypes
import concourse.bass as bass
import concourse.mybir as mybir
from concourse.bass_utils import run_bass_kernel_spmd
from concourse import library_config
from concourse.library_overlay import lower_extended_insts
from contextlib import ExitStack

F32 = mybir.dt.float32
BF16 = mybir.dt.bfloat16
I16 = mybir.dt.int16
ALU = mybir.AluOpType
AF = mybir.ActivationFunctionType
AX = mybir.AxisListType

B, L, D = 8, 4096, 1024
H, DK, DV, CH = 4, 256, 256, 32
NT = L // 128
NC = D // 128
E, HID, CAPL = 8, 4096, 1280
CAP = 8192
NSLOT = E * CAPL
RG, NRG = 1024, 4
DG, NDG = 256, 16
NCHG = DG // CH


class W:
    _ops = {"tensor_tensor", "tensor_scalar", "tensor_single_scalar",
            "scalar_tensor_tensor", "tensor_copy", "tensor_reduce", "reciprocal",
            "memset", "memzero", "tensor_tensor_scan", "select", "transpose",
            "iota", "tensor_add", "tensor_mul", "tensor_sub", "activation",
            "copy", "add", "mul", "sqrt", "square", "sign", "copy_predicated",
            "stream_shuffle"}

    def __init__(self, e):
        object.__setattr__(self, "_e", e)

    def __getattr__(self, k):
        a = getattr(object.__getattribute__(self, "_e"), k)
        if k in W._ops:
            e = object.__getattribute__(self, "_e")

            def call(*args, **kw):
                r = a(*args, **kw)
                e.drain()
                return r
            return call
        return a


class Ctx:
    pass


ALL_PHASES = ("p1", "p3", "p35", "p2", "p2b", "p4", "p5")


def build(taps=(), phases=ALL_PHASES, glim=None, slim=9):
    nc = bass.Bass("TRN2", target_bir_lowering=False)
    c = Ctx()
    c.nc = nc
    c.taps = set(taps)
    c.phases = set(phases)
    c.glim = NDG if glim is None else glim
    c.slim = slim
    c.stages = 99

    c.x_h = nc.dram_tensor("x", [L, D], F32, kind="ExternalInput")
    c.wq_h = nc.dram_tensor("wq", [D, D], F32, kind="ExternalInput")
    c.wk_h = nc.dram_tensor("wk", [D, D], F32, kind="ExternalInput")
    c.wv_h = nc.dram_tensor("wv", [D, D], F32, kind="ExternalInput")
    c.qcw_h = nc.dram_tensor("qcw", [128, NC, 4], F32, kind="ExternalInput")
    c.kcw_h = nc.dram_tensor("kcw", [128, NC, 4], F32, kind="ExternalInput")
    c.vcw_h = nc.dram_tensor("vcw", [128, NC, 4], F32, kind="ExternalInput")
    c.wb_h = nc.dram_tensor("wb", [D, H], F32, kind="ExternalInput")
    c.fir_h = nc.dram_tensor("fir", [128, NC, 56], F32, kind="ExternalInput")
    c.w1_h = nc.dram_tensor("w1", [6 * D, 2 * D], F32, kind="ExternalInput")
    c.b1_h = nc.dram_tensor("b1", [128, 16], F32, kind="ExternalInput")
    c.w2_h = nc.dram_tensor("w2", [2 * D, 20], F32, kind="ExternalInput")
    c.b2_h = nc.dram_tensor("b2", [20, 1], F32, kind="ExternalInput")
    c.idw_h = nc.dram_tensor("idw", [D, H], F32, kind="ExternalInput")
    c.idb_h = nc.dram_tensor("idb", [H, 1], F32, kind="ExternalInput")
    c.ltau_h = nc.dram_tensor("ltau", [H, 1], F32, kind="ExternalInput")
    c.onw_h = nc.dram_tensor("onw", [128, 2], F32, kind="ExternalInput")
    c.wo_h = nc.dram_tensor("wo", [D, D], F32, kind="ExternalInput")
    c.gate_h = nc.dram_tensor("gate", [D, E], F32, kind="ExternalInput")
    c.fc1t_h = nc.dram_tensor("fc1t", [E, 32, D, 128], F32, kind="ExternalInput")
    c.fb1_h = nc.dram_tensor("fb1", [128, E * 32], F32, kind="ExternalInput")
    c.fc2_h = nc.dram_tensor("fc2", [E, HID, D], F32, kind="ExternalInput")
    c.fb2_h = nc.dram_tensor("fb2", [E, D], F32, kind="ExternalInput")

    c.idf_h = nc.dram_tensor("idf", [128, 128], F32, kind="ExternalInput")
    c.idb16_h = nc.dram_tensor("idb16", [128, 128], BF16, kind="ExternalInput")
    c.iota8_h = nc.dram_tensor("iota8c", [128, 256], F32, kind="ExternalInput")
    c.iotap_h = nc.dram_tensor("iotapc", [128, 256], F32, kind="ExternalInput")
    c.eye32_h = nc.dram_tensor("eye32c", [32, 128], F32, kind="ExternalInput")
    c.eye32x2_h = nc.dram_tensor("eye32x2c", [32, 128], F32, kind="ExternalInput")
    c.slow32_h = nc.dram_tensor("slow32c", [32, 128], F32, kind="ExternalInput")
    c.low32_h = nc.dram_tensor("low32c", [32, 128], F32, kind="ExternalInput")
    c.onesb_h = nc.dram_tensor("onesbc", [8, 2], BF16, kind="ExternalInput")
    c.ones1r_h = nc.dram_tensor("ones1rc", [1, 128], BF16, kind="ExternalInput")
    c.colsel_h = nc.dram_tensor("colselc", [128, 16], BF16, kind="ExternalInput")
    c.premask_h = nc.dram_tensor("premaskc", [8, 2], F32, kind="ExternalInput")
    c.caplv_h = nc.dram_tensor("caplvc", [8, 1], F32, kind="ExternalInput")
    c.tokid_h = nc.dram_tensor("tokidc", [128, NT, 128], I16, kind="ExternalInput")
    c.ones128_h = nc.dram_tensor("ones128c", [128, 1], BF16, kind="ExternalInput")
    c.bcast4_h = nc.dram_tensor("bcast4c", [4, 512], BF16, kind="ExternalInput")
    c.slow128_h = nc.dram_tensor("slow128c", [128, 128], F32, kind="ExternalInput")
    c.eyeb128_h = nc.dram_tensor("eyeb128c", [128, 128], F32, kind="ExternalInput")
    c.eyeb2_h = nc.dram_tensor("eyeb2c", [128, 128], F32, kind="ExternalInput")
    c.lowi128_h = nc.dram_tensor("lowi128c", [128, 128], F32, kind="ExternalInput")
    c.e5_h = nc.dram_tensor("e5c", [4, 20], F32, kind="ExternalInput")

    c.xn_hbm = nc.dram_tensor("xn_hbm", [L, D], BF16, kind="Internal")
    c.xnf_hbm = nc.dram_tensor("xnf_hbm", [D, L], BF16, kind="Internal")
    c.v_hbm = nc.dram_tensor("v_hbm", [D, 32 + L], BF16, kind="Internal")
    c.delta_hbm = nc.dram_tensor("delta_hbm", [D, L], BF16, kind="Internal")
    c.eo_hbm = nc.dram_tensor("eo_hbm", [NSLOT, D], BF16, kind="Internal")
    c.w1bf_hbm = nc.dram_tensor("w1bf_hbm", [6 * D, 2 * D], BF16, kind="Internal")
    c.tbl_hbm = nc.dram_tensor("tbl_hbm", [NSLOT, 128], I16, kind="Internal")
    c.wrow_hbm = nc.dram_tensor("wrow_hbm", [4, L], I16, kind="Internal")
    c.wdisp_hbm = nc.dram_tensor("wdisp_hbm", [32, 640], I16, kind="Internal")
    c.cc_src = nc.dram_tensor("cc_src", [1, 16], F32, kind="Internal")
    c.cc_dst = nc.dram_tensor("cc_dst", [8, 16], F32, kind="Internal")

    c.out_h = nc.dram_tensor("out", [L, D], F32, kind="ExternalOutput")

    c.tap_h = {}

    def tapdecl(name, shape, dtype=F32):
        if name in c.taps:
            c.tap_h[name] = nc.dram_tensor("tap_" + name, shape, dtype, kind="ExternalOutput")
    tapdecl("logits", [128, NT * 8])
    tapdecl("top", [128, NT * 6])
    tapdecl("scan", [8, 4 * L])
    tapdecl("valid", [8, 2 * L], BF16)
    tapdecl("geff", [128, NT * 2])
    tapdecl("offs", [32, 1])
    tapdecl("wrow", [4, L], I16)
    tapdecl("wdisp", [32, 640], I16)
    c.sem_names = [
        "bar", "s_x", "s_w", "s_c", "s_stat", "s_var", "s_sqrt", "s_app",
        "s_mm", "s_evt", "s_mm2", "s_lg", "s_st", "s_gp", "s_cc", "s_vec",
        "s_act", "s_mm3", "s_evt2", "s_gp2", "s_ld1", "s_ld2", "s_ld3",
        "s_mm4", "s_vec2", "s_act2", "s_st2", "s_gp3", "s_mm5", "s_vec3",
        "s_fin", "s_x2", "s_st3", "s_3mm", "s_3ev", "s_3cv", "s_3si", "s_3st", "s_3w", "s_5a", "s_5b", "s_5c", "s_5d", "s_5e", "s_5f", "s_5g", "s_5st",
        "s_4w", "s_4a", "s_4h", "s_4eo", "s_4ev", "s_4st", "s_4s2", "s_4g",
        "s_4tp", "s_4td",
        "s_5ld", "s_5fa", "s_5bd", "s_5fe", "s_5r", "s_5hm", "s_5w1", "s_5fc",
        "s_5gt", "s_5pv", "s_5tm", "s_5lg", "s_5sf", "s_5on", "s_5oe", "s_5ob",
        "s_5os", "s_5xr", "s_5wc", "s_5gw", "s_5cb", "s_5w2", "s_5g2", "s_5cv",
    ]

    with ExitStack() as stk:
        c.sem = {}
        c.semn = {}
        for s in c.sem_names:
            c.sem[s] = stk.enter_context(nc.semaphore(s))
            c.semn[s] = 0

        def inc(name, instr, k=1):
            instr.then_inc(c.sem[name], k)
            c.semn[name] += k
            return instr
        c.inc = inc

        def n_of(name):
            return c.semn[name]
        c.n_of = n_of

        c.phase = 0

        def barrier(blk):
            need = 5 * c.phase
            for en in ("sync", "scalar", "vector", "tensor", "gpsimd"):
                def _mk(e, _need=need):
                    e.nop().then_inc(c.sem["bar"], 1)
                    e.wait_ge(c.sem["bar"], _need + 5)
                getattr(blk, en)(_mk)
            c.phase += 1
        c.barrier = barrier

        sb = lambda name, shape, dt: stk.enter_context(nc.sbuf_tensor(name, shape, dt))
        c.identf = sb("identf", [128, 128], F32)
        c.identb = sb("identb", [128, 128], BF16)
        c.ones128 = sb("ones128_s", [128, 1], BF16)
        c.geff0 = sb("geff0", [128, NT], F32)
        c.geff1 = sb("geff1", [128, NT], F32)
        c.gtfm = sb("gtfm", [8, NT * 128], BF16)
        c.w0w = sb("w0w", [128, 256], I16)
        c.w1w = sb("w1w", [128, 256], I16)
        c.wboth = sb("wboth", [128, 512], I16)
        c.fb2sb = sb("fb2sb", [8, D], BF16)
        c.b1sb = sb("b1sb", [128, 16], F32)

        with ExitStack() as g2:
            sb2 = lambda name, shape, dt: g2.enter_context(nc.sbuf_tensor(name, shape, dt))
            c.iota8c = sb2("iota8c_s", [128, 256], F32)
            c.iotapc = sb2("iotapc_s", [128, 256], F32)
            c.onesb = sb2("onesb_s", [8, 2], BF16)
            c.ones1r = sb2("ones1r_s", [1, 128], BF16)
            c.colsel = sb2("colsel_s", [128, 16], BF16)
            c.premask = sb2("premask_s", [8, 2], F32)
            c.caplv = sb2("caplv_s", [8, 1], F32)
            c.bcast4 = sb2("bcast4_s", [4, 512], BF16)
            c.slow128 = sb2("slow128_s", [128, 128], F32)
            c.eyeb128 = sb2("eyeb128_s", [128, 128], F32)
            c.eyeb2 = sb2("eyeb2_s", [128, 128], F32)
            c.lowi128 = sb2("lowi128_s", [128, 128], F32)
            c.eye32 = sb2("eye32_s", [32, 128], F32)
            c.eye32x2 = sb2("eye32x2_s", [32, 128], F32)
            c.slow32 = sb2("slow32_s", [32, 128], F32)
            c.low32 = sb2("low32_s", [32, 128], F32)
            c.wdisp = sb2("wdisp", [128, 640], I16)
            c.lgtm = sb2("lgtm", [128, NT, 8], F32)
            c.fb1sb = sb2("fb1sb", [128, E * 32], F32)

            with ExitStack() as xstk:
                c.xn_fm = xstk.enter_context(nc.sbuf_tensor("xn_fm", [128, NC, L], BF16))
                _p1(c)
                if "p3" in c.phases:
                    _p3(c)
                if "p35" in c.phases:
                    _p35(c)
            if "p2" in c.phases:
                _p2(c)
            if "p2b" in c.phases:
                _p2b(c)
            if "p4" in c.phases:
                _p4(c)
        if "p5" in c.phases:
            _p5(c)
        _fin(c)
    return nc


def _p1(c):
    nc, inc, n_of = c.nc, c.inc, c.n_of
    with ExitStack() as ps:
        xtile = ps.enter_context(nc.sbuf_tensor("xtile", [128, 2, D], F32))
        junk = ps.enter_context(nc.sbuf_tensor("junk", [128, 2, D], F32))
        vtmp = ps.enter_context(nc.sbuf_tensor("vtmp", [128, 2], F32))
        xntm = ps.enter_context(nc.sbuf_tensor("xntm", [128, 2, D], BF16))
        stat = ps.enter_context(nc.sbuf_tensor("stat", [128, NT, 2], F32))
        nm = ps.enter_context(nc.sbuf_tensor("nm", [128, NT], F32))
        rr = ps.enter_context(nc.sbuf_tensor("rr", [128, NT], F32))
        rr2 = ps.enter_context(nc.sbuf_tensor("rr2", [128, NT], F32))
        gatesb = ps.enter_context(nc.sbuf_tensor("gatesb", [128, NC, E], BF16))
        lgfm = ps.enter_context(nc.sbuf_tensor("lgfm", [8, L], F32))
        w1cast = ps.enter_context(nc.sbuf_tensor("w1cast", [128, 2048], BF16))
        pst = ps.enter_context(nc.psum_tensor("pst", [128, 2, 1024], BF16))
        plg = ps.enter_context(nc.psum_tensor("plg", [8, 512], F32))
        plgt = ps.enter_context(nc.psum_tensor("plgt", [128, 64], F32))
        blk = ps.enter_context(nc.Block())

        @blk.gpsimd
        def _(g):
            g.load_library(library_config.mlp)
            for src, dst in [(c.idf_h, c.identf), (c.idb16_h, c.identb),
                             (c.iota8_h, c.iota8c), (c.iotap_h, c.iotapc),
                             (c.onesb_h, c.onesb), (c.ones1r_h, c.ones1r),
                             (c.colsel_h, c.colsel),
                             (c.premask_h, c.premask), (c.caplv_h, c.caplv),
                             (c.eye32_h, c.eye32), (c.eye32x2_h, c.eye32x2),
                             (c.ones128_h, c.ones128), (c.bcast4_h, c.bcast4),
                             (c.slow128_h, c.slow128),
                             (c.eyeb128_h, c.eyeb128), (c.eyeb2_h, c.eyeb2),
                             (c.lowi128_h, c.lowi128),
                             (c.slow32_h, c.slow32), (c.low32_h, c.low32),
                             (c.b1_h, c.b1sb), (c.fb1_h, c.fb1sb)]:
                inc("s_c", g.dma_start(dst[:], src[:]), 16)
            inc("s_c", g.dma_start(
                gatesb[:, :, :], c.gate_h.ap().rearrange("(kc p) e -> p kc e", p=128)), 16)
            inc("s_c", g.dma_start(c.fb2sb[:, :], c.fb2_h[:, :]), 16)
            c.sc_total = n_of("s_c")
            w1r = c.w1_h.ap().rearrange("(i p) m -> i p m", p=128)
            w1w = c.w1bf_hbm.ap().rearrange("(i p) m -> i p m", p=128)
            for i in range(48):
                inc("s_w", g.dma_start(w1cast[:, :], w1r[i, :, :]), 16)
                g.wait_ge(c.sem["s_w"], n_of("s_w"))
                inc("s_gp", g.dma_start(w1w[i, :, :], w1cast[:, :]), 16)
                g.wait_ge(c.sem["s_gp"], n_of("s_gp"))

        @blk.sync
        def _(sy):
            xv = c.x_h.ap().rearrange("(t p) d -> t p d", p=128)
            for t in range(NT):
                if t >= 2:
                    sy.wait_ge(c.sem["s_app"], t - 1)
                inc("s_x" if t % 2 == 0 else "s_x2",
                    sy.dma_start(xtile[:, t % 2, :], xv[t, :, :]), 16)
            sy.wait_ge(c.sem["s_evt"], NT)
            for kc in range(NC):
                inc("s_x", sy.dma_start(
                    c.xnf_hbm[kc * 128:(kc + 1) * 128, :], c.xn_fm[:, kc, :]), 16)
            sy.wait_ge(c.sem["s_x"], n_of("s_x"))
            sy.wait_ge(c.sem["s_x2"], n_of("s_x2"))

        @blk.scalar
        def _(sc):
            sc = W(sc)
            xnv = c.xn_hbm.ap().rearrange("(t p) d -> t p d", p=128)
            for t in range(NT):
                sc.wait_ge(c.sem["s_x" if t % 2 == 0 else "s_x2"], 16 * (t // 2 + 1))
                if t >= 1:
                    sc.wait_ge(c.sem["s_stat"], t)
                sc.activation(junk[:, 0, :], xtile[:, t % 2, :], AF.Copy,
                              accum_out=stat[:, t, 0:1])
                inc("s_stat", sc.activation(junk[:, 1, :], xtile[:, t % 2, :], AF.Square,
                                            accum_out=stat[:, t, 1:2]), 1)
                if t >= 1:
                    sc.wait_ge(c.sem["s_var"], t)
                    inc("s_sqrt", sc.activation(rr2[:, t - 1:t], rr[:, t - 1:t], AF.Sqrt), 1)
                if t >= 2:
                    tb = t - 2
                    sc.wait_ge(c.sem["s_app"], tb + 1)
                    inc("s_st" if tb % 2 == 0 else "s_st3",
                        sc.dma_start(xnv[tb, :, :], xntm[:, tb % 2, :]), 16)
            sc.wait_ge(c.sem["s_var"], NT)
            inc("s_sqrt", sc.activation(rr2[:, NT - 1:NT], rr[:, NT - 1:NT], AF.Sqrt), 1)
            for tb in range(NT - 2, NT):
                sc.wait_ge(c.sem["s_app"], tb + 1)
                inc("s_st" if tb % 2 == 0 else "s_st3",
                    sc.dma_start(xnv[tb, :, :], xntm[:, tb % 2, :]), 16)
            sc.wait_ge(c.sem["s_st"], n_of("s_st"))
            sc.wait_ge(c.sem["s_st3"], n_of("s_st3"))

        @blk.vector
        def _(v):
            v = W(v)
            for t in range(NT + 2):
                if t < NT:
                    v.wait_ge(c.sem["s_stat"], t + 1)
                    v.tensor_scalar(nm[:, t:t + 1], stat[:, t, 0:1], -1.0 / D, None, ALU.mult)
                    v.tensor_scalar(rr[:, t:t + 1], stat[:, t, 1:2], 1.0 / D, None, ALU.mult)
                    v.drain()
                    v.scalar_tensor_tensor(vtmp[:, 0:1], nm[:, t:t + 1], nm[:, t:t + 1],
                                           rr[:, t:t + 1], ALU.mult, ALU.subtract)
                    v.drain()
                    v.tensor_scalar(vtmp[:, 1:2], vtmp[:, 0:1], -1.0, 1e-5, ALU.mult, ALU.add)
                    v.drain()
                    v.reciprocal(rr[:, t:t + 1], vtmp[:, 1:2])
                    v.drain()
                    inc("s_var", v.nop(), 1)
                if 1 <= t <= NT:
                    ta = t - 1
                    v.wait_ge(c.sem["s_sqrt"], ta + 1)
                    if ta >= 2:
                        v.wait_ge(c.sem["s_mm"], ta - 1)
                        v.wait_ge(c.sem["s_st" if ta % 2 == 0 else "s_st3"],
                                  16 * ((ta - 2) // 2 + 1))
                    inc("s_app", v.tensor_scalar(xntm[:, ta % 2, :], xtile[:, ta % 2, :],
                                                 nm[:, ta:ta + 1], rr2[:, ta:ta + 1],
                                                 ALU.add, ALU.mult), 1)
                if t >= 2:
                    tb = t - 2
                    v.wait_ge(c.sem["s_mm"], tb + 1)
                    src = bass.AP(pst, (tb % 2) * 1024, [[2048, 128], [128, NC], [1, 128]])
                    dst = bass.AP(c.xn_fm, tb * 128, [[NC * L, 128], [L, NC], [1, 128]])
                    inc("s_evt", v.tensor_copy(dst, src), 1)
            for tt in range(8):
                v.wait_ge(c.sem["s_mm2"], tt + 1)
                inc("s_lg", v.tensor_copy(lgfm[:, tt * 512:(tt + 1) * 512], plg[:, :]), 1)
            for grp in range(4):
                v.wait_ge(c.sem["s_mm2"], 8 + grp + 1)
                src = bass.AP(plgt, 0, [[64, 128], [1, 64]])
                dst = bass.AP(c.lgtm, grp * 64, [[NT * 8, 128], [1, 64]])
                inc("s_lg", v.tensor_copy(dst, src), 1)

        @blk.tensor
        def _(te):
            te.wait_ge(c.sem["s_c"], c.sc_total)
            for t in range(NT):
                te.wait_ge(c.sem["s_app"], t + 1)
                if t >= 2:
                    te.wait_ge(c.sem["s_evt"], t - 1)
                for kc in range(NC):
                    ins = te.transpose(pst[:, (t % 2), kc * 128:(kc + 1) * 128],
                                       xntm[:, t % 2, kc * 128:(kc + 1) * 128], c.identb[:, :])
                inc("s_mm", ins, 1)
            te.wait_ge(c.sem["s_evt"], NT)
            for tt in range(8):
                if tt >= 1:
                    te.wait_ge(c.sem["s_lg"], tt)
                for kc in range(NC):
                    ins = te.matmul(plg[:, :], gatesb[:, kc, :],
                                    c.xn_fm[:, kc, tt * 512:(tt + 1) * 512],
                                    start=(kc == 0), stop=(kc == NC - 1))
                inc("s_mm2", ins, 1)
            te.wait_ge(c.sem["s_lg"], 8)
            for grp in range(4):
                if grp >= 1:
                    te.wait_ge(c.sem["s_lg"], 8 + grp)
                for j in range(8):
                    t = grp * 8 + j
                    ins = te.transpose(plgt[:, j * 8:(j + 1) * 8],
                                       lgfm[:, t * 128:(t + 1) * 128], c.identf[0:8, 0:8])
                inc("s_mm2", ins, 1)


def _p2(c):
    nc, inc, n_of = c.nc, c.inc, c.n_of
    with ExitStack() as ps:
        c.sc0 = ps.enter_context(nc.sbuf_tensor("sc0", [8, L], F32))
        c.sc1 = ps.enter_context(nc.sbuf_tensor("sc1", [8, L], F32))
        m0a = ps.enter_context(nc.sbuf_tensor("m0a", [128, NT], F32))
        m1a = ps.enter_context(nc.sbuf_tensor("m1a", [128, NT], F32))
        i0a = ps.enter_context(nc.sbuf_tensor("i0a", [128, NT], F32))
        i1a = ps.enter_context(nc.sbuf_tensor("i1a", [128, NT], F32))
        oh0 = ps.enter_context(nc.sbuf_tensor("oh0", [128, NT, 8], BF16))
        oh1 = ps.enter_context(nc.sbuf_tensor("oh1", [128, NT, 8], BF16))
        tmpa = ps.enter_context(nc.sbuf_tensor("tmpa", [128, NT, 8], F32))
        tmpb = ps.enter_context(nc.sbuf_tensor("tmpb", [128, NT, 8], F32))
        g0 = ps.enter_context(nc.sbuf_tensor("g0", [128, NT], F32))
        g1 = ps.enter_context(nc.sbuf_tensor("g1", [128, NT], F32))
        zfm = ps.enter_context(nc.sbuf_tensor("zfm", [8, L], BF16))
        m0fm = ps.enter_context(nc.sbuf_tensor("m0fm", [8, L], BF16))
        m1fm = ps.enter_context(nc.sbuf_tensor("m1fm", [8, L], BF16))
        m0w = ps.enter_context(nc.sbuf_tensor("m0w", [8, L], BF16))
        m1w = ps.enter_context(nc.sbuf_tensor("m1w", [8, L], BF16))
        cpack = ps.enter_context(nc.sbuf_tensor("cpack", [8, 2], F32))
        cps = ps.enter_context(nc.sbuf_tensor("cps", [2, 8], F32))
        ccrow = ps.enter_context(nc.sbuf_tensor("ccrow", [1, 16], F32))
        agsb = ps.enter_context(nc.sbuf_tensor("agsb", [8, 16], F32))
        prefsb = ps.enter_context(nc.sbuf_tensor("prefsb", [1, 32], F32))
        offcb = ps.enter_context(nc.sbuf_tensor("offcb", [8, 4], F32))
        thr0 = ps.enter_context(nc.sbuf_tensor("thr0", [8, 1], F32))
        thr1 = ps.enter_context(nc.sbuf_tensor("thr1", [8, 1], F32))
        po1 = ps.enter_context(nc.sbuf_tensor("po1", [8, 1], F32))
        sc0l = ps.enter_context(nc.sbuf_tensor("sc0l", [8, 1], F32))
        vld0 = ps.enter_context(nc.sbuf_tensor("vld0", [8, L], BF16))
        vld1 = ps.enter_context(nc.sbuf_tensor("vld1", [8, L], BF16))
        vmm = ps.enter_context(nc.sbuf_tensor("vmm", [8, L], BF16))
        vrowb = ps.enter_context(nc.sbuf_tensor("vrowb", [33, L], BF16))
        v0tm = ps.enter_context(nc.sbuf_tensor("v0tm", [128, NT], F32))
        v1tm = ps.enter_context(nc.sbuf_tensor("v1tm", [128, NT], F32))
        gttb = ps.enter_context(nc.sbuf_tensor("gttb", [128, 2, 8], BF16))
        srowi = ps.enter_context(nc.sbuf_tensor("srowi", [1, L], I16))
        ptrA = ps.enter_context(nc.psum_tensor("ptrA", [8, 256], BF16))
        ptrB = ps.enter_context(nc.psum_tensor("ptrB", [8, 256], BF16))
        pmisc = ps.enter_context(nc.psum_tensor("pmisc", [8, 96], F32))
        pvr = ps.enter_context(nc.psum_tensor("pvr", [1, 2, 512], F32))
        pvt = ps.enter_context(nc.psum_tensor("pvt", [128, 128], BF16))
        pgtA = ps.enter_context(nc.psum_tensor("pgtA", [8, 128], BF16))
        pgtB = ps.enter_context(nc.psum_tensor("pgtB", [8, 128], BF16))
        blk = ps.enter_context(nc.Block())

        c.barrier(blk)

        @blk.vector
        def _(v):
            v = W(v)
            v.memset(c.w0w[:, :], 0)
            v.memset(c.w1w[:, :], 0)
            v.memset(c.wboth[:, :], 0)
            v.memset(c.wdisp[:, :], 0)
            lg3 = c.lgtm[:, :, :]
            m0b = bass.AP(m0a, 0, [[NT, 128], [1, NT], [0, 8]])
            m1b = bass.AP(m1a, 0, [[NT, 128], [1, NT], [0, 8]])
            i0b = bass.AP(i0a, 0, [[NT, 128], [1, NT], [0, 8]])
            i1b = bass.AP(i1a, 0, [[NT, 128], [1, NT], [0, 8]])
            io3 = c.iota8c.ap().rearrange("p (t e) -> p t e", e=8)
            iop3 = c.iotapc.ap().rearrange("p (t e) -> p t e", e=8)
            v.tensor_reduce(m0a[:, :], lg3, AX.X, ALU.max)
            v.drain()
            v.tensor_tensor(tmpb[:, :, :], lg3, m0b, ALU.is_equal)
            v.scalar_tensor_tensor(tmpa[:, :, :], tmpb[:, :, :], -1e4, iop3, ALU.mult, ALU.add)
            v.tensor_reduce(i0a[:, :], tmpa[:, :, :], AX.X, ALU.min)
            v.drain()
            v.tensor_tensor(oh0[:, :, :], io3, i0b, ALU.is_equal)
            v.drain()
            v.scalar_tensor_tensor(tmpb[:, :, :], oh0[:, :, :], -2e9, lg3, ALU.mult, ALU.add)
            v.tensor_reduce(m1a[:, :], tmpb[:, :, :], AX.X, ALU.max)
            v.drain()
            v.tensor_tensor(tmpa[:, :, :], tmpb[:, :, :], m1b, ALU.is_equal)
            v.scalar_tensor_tensor(tmpb[:, :, :], tmpa[:, :, :], -1e4, iop3, ALU.mult, ALU.add)
            v.tensor_reduce(i1a[:, :], tmpb[:, :, :], AX.X, ALU.min)
            v.drain()
            v.tensor_tensor(oh1[:, :, :], io3, i1b, ALU.is_equal)
            v.tensor_tensor(g0[:, :], m0a[:, :], m1a[:, :], ALU.subtract)
            inc("s_vec", v.nop(), 1)

        @blk.scalar
        def _(sc):
            sc = W(sc)
            sc.wait_ge(c.sem["s_vec"], n_of("s_vec"))
            inc("s_act", sc.activation(g0[:, :], g0[:, :], AF.Sigmoid), 1)

        @blk.vector
        def _(v):
            v = W(v)
            v.wait_ge(c.sem["s_act"], n_of("s_act"))
            v.tensor_scalar(g1[:, :], g0[:, :], -1.0, 1.0, ALU.mult, ALU.add)
            inc("s_vec", v.nop(), 1)

        mm3_0 = n_of("s_mm3")
        evt_0 = n_of("s_evt")
        vec_marks = n_of("s_vec")

        @blk.tensor
        def _(te):
            te.wait_ge(c.sem["s_vec"], vec_marks)
            for t in range(NT):
                if t >= 2:
                    te.wait_ge(c.sem["s_evt"], evt_0 + t - 1)
                bank = ptrA if t % 2 == 0 else ptrB
                te.transpose(bank[:, 0:128], oh0[:, t, :], c.identb[:, :])
                ins = te.transpose(bank[:, 128:256], oh1[:, t, :], c.identb[:, :])
                inc("s_mm3", ins, 1)

        @blk.vector
        def _(v):
            v = W(v)
            for t in range(NT):
                v.wait_ge(c.sem["s_mm3"], mm3_0 + t + 1)
                bank = ptrA if t % 2 == 0 else ptrB
                for j, (fmd, wd) in enumerate([(m0fm, m0w), (m1fm, m1w)]):
                    src = bass.AP(bank, j * 128, [[256, 8], [1, 128]])
                    dst = bass.AP(fmd, t * 128, [[L, 8], [1, 128]])
                    v.tensor_copy(dst, src)
                    srcs = bass.AP(bank, j * 128, [[256, 8], [1, 16], [16, 8]])
                    dsts = bass.AP(wd, t * 8, [[L, 8], [256, 16], [1, 8]])
                    v.tensor_copy(dsts, srcs)
                inc("s_evt", v.nop(), 1)
            v.memset(zfm[:, :], 0.0)
            v.tensor_tensor_scan(c.sc0[:, :], m0fm[:, :], zfm[:, :], 0.0, ALU.add, ALU.add)
            v.tensor_tensor_scan(c.sc1[:, :], m1fm[:, :], zfm[:, :], 0.0, ALU.add, ALU.add)
            v.tensor_copy(cpack[:, 0:1], c.sc0[:, L - 1:L])
            v.tensor_copy(cpack[:, 1:2], c.sc1[:, L - 1:L])
            inc("s_vec", v.nop(), 1)

        m_cnts = n_of("s_vec")
        mm3_1 = n_of("s_mm3")

        @blk.tensor
        def _(te):
            te.wait_ge(c.sem["s_vec"], m_cnts)
            te.transpose(pmisc[0:1, 0:8], cpack[:, 0:1], c.identf[0:8, 0:8])
            ins = te.transpose(pmisc[0:1, 8:16], cpack[:, 1:2], c.identf[0:8, 0:8])
            inc("s_mm3", ins, 1)

        @blk.vector
        def _(v):
            v = W(v)
            v.wait_ge(c.sem["s_mm3"], mm3_1 + 1)
            v.tensor_copy(ccrow[:, :], pmisc[0:1, 0:16])
            inc("s_vec", v.nop(), 1)

        m_ccrow = n_of("s_vec")
        gp2_0 = n_of("s_gp2")

        @blk.gpsimd
        def _(g):
            g.wait_ge(c.sem["s_vec"], m_ccrow)
            inc("s_gp", g.dma_start(c.cc_src[:, :], ccrow[:, :]), 16)
            g.wait_ge(c.sem["s_gp"], n_of("s_gp"))
            inc("s_cc", g.collective_compute(
                "AllGather", ALU.bypass, replica_groups=[[0, 1, 2, 3, 4, 5, 6, 7]],
                ins=[c.cc_src[:, :]], outs=[c.cc_dst[:, :]]), 1)
            g.wait_ge(c.sem["s_cc"], n_of("s_cc"))
            inc("s_gp2", g.dma_start(agsb[:, :], c.cc_dst[:, :]), 16)

        mm3_2 = n_of("s_mm3")

        @blk.tensor
        def _(te):
            te.wait_ge(c.sem["s_gp2"], gp2_0 + 16)
            te.matmul(pmisc[0:1, 32:48], c.premask[:, 0:1], agsb[:, :], start=True, stop=True)
            ins = te.matmul(pmisc[0:1, 48:64], c.premask[:, 1:2], agsb[:, :], start=True, stop=True)
            inc("s_mm3", ins, 1)

        @blk.vector
        def _(v):
            v = W(v)
            v.wait_ge(c.sem["s_mm3"], mm3_2 + 1)
            v.tensor_copy(prefsb[:, :], pmisc[0:1, 32:64])
            inc("s_vec", v.nop(), 1)

        m_pref = n_of("s_vec")
        mm3_3 = n_of("s_mm3")

        @blk.tensor
        def _(te):
            te.wait_ge(c.sem["s_vec"], m_pref)
            te.transpose(pmisc[0:8, 64:65], prefsb[0:1, 0:8], c.identf[0:1, 0:1])
            te.transpose(pmisc[0:8, 65:66], prefsb[0:1, 8:16], c.identf[0:1, 0:1])
            ins = te.transpose(pmisc[0:8, 66:67], prefsb[0:1, 16:24], c.identf[0:1, 0:1])
            inc("s_mm3", ins, 1)

        @blk.vector
        def _(v):
            v = W(v)
            v.wait_ge(c.sem["s_mm3"], mm3_3 + 1)
            v.tensor_copy(offcb[:, 0:1], pmisc[0:8, 64:65])
            v.tensor_copy(offcb[:, 1:2], pmisc[0:8, 65:66])
            v.tensor_copy(offcb[:, 2:3], pmisc[0:8, 66:67])
            v.drain()
            v.tensor_scalar(thr0[:, :], offcb[:, 0:1], -1.0, CAP + 0.5, ALU.mult, ALU.add)
            v.tensor_tensor(po1[:, :], offcb[:, 1:2], offcb[:, 2:3], ALU.add)
            v.drain()
            v.tensor_scalar(thr1[:, :], po1[:, :], -1.0, CAP + 0.5, ALU.mult, ALU.add)
            v.drain()
            inc("s_vec", v.nop(), 1)

        for j in range(2):
            m_thr = n_of("s_vec")
            mm4_0 = n_of("s_mm4")
            evt2_0 = n_of("s_evt2")
            scj = c.sc0 if j == 0 else c.sc1
            mfj = m0fm if j == 0 else m1fm
            thj = thr0 if j == 0 else thr1

            vldj = vld0 if j == 0 else vld1

            @blk.vector
            def _(v, scj=scj, mfj=mfj, thj=thj, m_thr=m_thr, evt2_0=evt2_0, vldj=vldj):
                v = W(v)
                v.wait_ge(c.sem["s_vec"], m_thr)
                if evt2_0 > 0:
                    v.wait_ge(c.sem["s_evt2"], evt2_0)
                v.tensor_single_scalar(vldj[:, :], scj[:, :], thj[:, :], ALU.is_lt)
                v.tensor_tensor(vmm[:, :], mfj[:, :], vldj[:, :], ALU.mult)
                inc("s_vec", v.nop(), 1)

            m_vmm = n_of("s_vec")

            @blk.tensor
            def _(te, j=j, m_vmm=m_vmm, mm4_0=mm4_0, evt2_0=evt2_0):
                te.wait_ge(c.sem["s_vec"], m_vmm)
                for tt in range(8):
                    if tt >= 2:
                        te.wait_ge(c.sem["s_evt2"], evt2_0 + tt - 1)
                    inc("s_mm4", te.matmul(pvr[0:1, tt % 2, :], c.onesb[:, 0:1],
                                           vmm[:, tt * 512:(tt + 1) * 512],
                                           start=True, stop=True), 1)

            @blk.vector
            def _(v, j=j, mm4_0=mm4_0):
                v = W(v)
                for tt in range(8):
                    v.wait_ge(c.sem["s_mm4"], mm4_0 + tt + 1)
                    inc("s_evt2", v.tensor_copy(vrowb[32 * j:32 * j + 1, tt * 512:(tt + 1) * 512],
                                                pvr[0:1, tt % 2, :]), 1)

        evt2_f = n_of("s_evt2")
        mm4_1 = n_of("s_mm4")

        @blk.tensor
        def _(te):
            te.wait_ge(c.sem["s_evt2"], evt2_f)
            for j in range(2):
                for t in range(NT):
                    ins = te.transpose(pvt[:, 2 * (j * NT + t):2 * (j * NT + t) + 1],
                                       vrowb[32 * j:32 * j + 1, t * 128:(t + 1) * 128],
                                       c.identb[32 * j:32 * j + 1, 32 * j:32 * j + 1])
            inc("s_mm4", ins, 1)

        @blk.vector
        def _(v):
            v = W(v)
            v.wait_ge(c.sem["s_mm4"], mm4_1 + 1)
            v.tensor_copy(v0tm[:, :], bass.AP(pvt, 0, [[128, 128], [2, NT]]))
            v.tensor_copy(v1tm[:, :], bass.AP(pvt, 2 * NT, [[128, 128], [2, NT]]))
            v.drain()
            v.tensor_tensor(c.geff0[:, :], g0[:, :], v0tm[:, :], ALU.mult)
            v.tensor_tensor(c.geff1[:, :], g1[:, :], v1tm[:, :], ALU.mult)
            v.drain()
            inc("s_vec", v.nop(), 1)

        m_geff = n_of("s_vec")
        mm5_0 = n_of("s_mm5")
        vec3_0 = n_of("s_vec3")
        evt3_0 = n_of("s_evt")

        @blk.vector
        def _(v):
            v = W(v)
            v.wait_ge(c.sem["s_vec"], m_geff)
            for t in range(NT + 1):
                if t < NT:
                    if t >= 2:
                        v.wait_ge(c.sem["s_mm5"], mm5_0 + t - 1)
                    v.tensor_scalar(tmpa[:, 0, :], oh0[:, t, :], c.geff0[:, t:t + 1],
                                    None, ALU.mult)
                    inc("s_vec3", v.scalar_tensor_tensor(
                        gttb[:, t % 2, :], oh1[:, t, :], c.geff1[:, t:t + 1],
                        tmpa[:, 0, :], ALU.mult, ALU.add), 1)
                if t >= 1:
                    tb = t - 1
                    v.wait_ge(c.sem["s_mm5"], mm5_0 + tb + 1)
                    bank = pgtA if tb % 2 == 0 else pgtB
                    inc("s_evt", v.tensor_copy(c.gtfm[:, tb * 128:(tb + 1) * 128],
                                               bank[:, :]), 1)

        @blk.tensor
        def _(te):
            for t in range(NT):
                te.wait_ge(c.sem["s_vec3"], vec3_0 + t + 1)
                if t >= 2:
                    te.wait_ge(c.sem["s_evt"], evt3_0 + t - 1)
                inc("s_mm5", te.transpose(pgtA[:, :] if t % 2 == 0 else pgtB[:, :],
                                          gttb[:, t % 2, :], c.identb[:, :]), 1)

        c.barrier(blk)

        sfm = ps.enter_context(nc.sbuf_tensor("sfm", [8, L], F32))

        @blk.vector
        def _(v):
            v = W(v)
            v.tensor_tensor_scan(c.sc0[:, :], m0w[:, :], zfm[:, :], 0.0, ALU.add, ALU.add)
            v.tensor_tensor_scan(c.sc1[:, :], m1w[:, :], zfm[:, :], 0.0, ALU.add, ALU.add)
            v.tensor_copy(sc0l[:, :], c.sc0[:, L - 1:L])
            v.drain()
            v.tensor_tensor(po1[:, :], c.caplv[:, :], sc0l[:, :], ALU.add)
            v.drain()
            inc("s_vec", v.nop(), 1)

        m_sw = n_of("s_vec")

        for j in range(2):
            mm4_2 = n_of("s_mm4")
            evt2_2 = n_of("s_evt2")
            vec2_0 = n_of("s_vec2")
            gp3_0 = n_of("s_gp3")
            scj = c.sc0 if j == 0 else c.sc1
            mwj = m0w if j == 0 else m1w
            base = None

            @blk.vector
            def _(v, j=j, scj=scj, mwj=mwj, m_sw=m_sw, gp3_0=gp3_0):
                v = W(v)
                v.wait_ge(c.sem["s_vec"], m_sw)
                if j == 1:
                    v.wait_ge(c.sem["s_gp3"], gp3_0)
                sca = c.caplv[:, :] if j == 0 else po1[:, :]
                v.scalar_tensor_tensor(sfm[:, :], scj[:, :], sca, mwj[:, :],
                                       ALU.add, ALU.mult)
                inc("s_vec", v.nop(), 1)

            m_sfm = n_of("s_vec")

            @blk.tensor
            def _(te, m_sfm=m_sfm, evt2_2=evt2_2):
                te.wait_ge(c.sem["s_vec"], m_sfm)
                for tt in range(8):
                    if tt >= 2:
                        te.wait_ge(c.sem["s_evt2"], evt2_2 + tt - 1)
                    inc("s_mm4", te.matmul(pvr[0:1, tt % 2, :], c.premask[:, 1:2],
                                           sfm[:, tt * 512:(tt + 1) * 512],
                                           start=True, stop=True), 1)

            @blk.vector
            def _(v, mm4_2=mm4_2):
                v = W(v)
                for tt in range(8):
                    v.wait_ge(c.sem["s_mm4"], mm4_2 + tt + 1)
                    inc("s_evt2", v.tensor_copy(srowi[0:1, tt * 512:(tt + 1) * 512],
                                                pvr[0:1, tt % 2, :]), 1)
                inc("s_vec2", v.nop(), 1)

            @blk.sync
            def _(sy, j=j, vec2_0=vec2_0):
                sy.wait_ge(c.sem["s_vec2"], vec2_0 + 1)
                inc("s_st2", sy.dma_start(c.wrow_hbm[2 * j:2 * j + 1, :], srowi[:, :]), 16)
                inc("s_st2", sy.dma_start(c.wrow_hbm[2 * j + 1:2 * j + 2, :], srowi[:, :]), 16)
                sy.wait_ge(c.sem["s_st2"], n_of("s_st2"))
                inc("s_gp3", sy.nop(), 1)

        @blk.sync
        def _(sy):
            wr = c.wrow_hbm.ap().rearrange("a (p cv) -> (a p) cv", cv=256)
            inc("s_st2", sy.dma_start(c.w0w[0:32, :], wr[0:32, :]), 16)
            inc("s_st2", sy.dma_start(c.w1w[0:32, :], wr[32:64, :]), 16)
            inc("s_st2", sy.dma_start(
                bass.AP(c.wboth, 0, [[512, 32], [32, 16], [1, 16]]),
                bass.AP(c.wrow_hbm, 0, [[256, 32], [16, 16], [1, 16]])), 16)
            inc("s_st2", sy.dma_start(
                bass.AP(c.wboth, 16, [[512, 32], [32, 16], [1, 16]]),
                bass.AP(c.wrow_hbm, 2 * L, [[256, 32], [16, 16], [1, 16]])), 16)
            sy.wait_ge(c.sem["s_st2"], n_of("s_st2"))

        if c.tap_h:
            m_all = n_of("s_vec")

            @blk.sync
            def _(sy):
                sy.wait_ge(c.sem["s_vec"], m_all)
                th = c.tap_h
                if "logits" in th:
                    inc("s_st2", sy.dma_start(th["logits"][:, :], c.lgtm[:, :, :]), 16)
                if "top" in th:
                    inc("s_st2", sy.dma_start(th["top"][:, 0:NT], i0a[:, :]), 16)
                    inc("s_st2", sy.dma_start(th["top"][:, NT:2 * NT], i1a[:, :]), 16)
                    inc("s_st2", sy.dma_start(th["top"][:, 2 * NT:3 * NT], g0[:, :]), 16)
                    inc("s_st2", sy.dma_start(th["top"][:, 3 * NT:4 * NT], g1[:, :]), 16)
                    inc("s_st2", sy.dma_start(th["top"][:, 4 * NT:5 * NT], m0a[:, :]), 16)
                    inc("s_st2", sy.dma_start(th["top"][:, 5 * NT:6 * NT], m1a[:, :]), 16)
                if "scan" in th:
                    inc("s_st2", sy.dma_start(th["scan"][:, 2 * L:3 * L], c.sc0[:, :]), 16)
                    inc("s_st2", sy.dma_start(th["scan"][:, 3 * L:4 * L], c.sc1[:, :]), 16)
                if "valid" in th:
                    inc("s_st2", sy.dma_start(th["valid"][:, 0:L], vld0[:, :]), 16)
                    inc("s_st2", sy.dma_start(th["valid"][:, L:2 * L], vld1[:, :]), 16)
                if "geff" in th:
                    inc("s_st2", sy.dma_start(th["geff"][:, 0:NT], c.geff0[:, :]), 16)
                    inc("s_st2", sy.dma_start(th["geff"][:, NT:2 * NT], c.geff1[:, :]), 16)
                if "offs" in th:
                    inc("s_st2", sy.dma_start(th["offs"][0:8, :], offcb[:, 0:1]), 16)
                    inc("s_st2", sy.dma_start(th["offs"][8:16, :], offcb[:, 1:2]), 16)
                    inc("s_st2", sy.dma_start(th["offs"][16:24, :], offcb[:, 2:3]), 16)
                sy.wait_ge(c.sem["s_st2"], n_of("s_st2"))


def _p2b(c):
    nc, inc, n_of = c.nc, c.inc, c.n_of
    with ExitStack() as ps:
        zi16 = ps.enter_context(nc.sbuf_tensor("zi16", [128, 1280], I16))
        tokid_s = ps.enter_context(nc.sbuf_tensor("tokid_s", [128, NT, 128], I16))
        wdb = ps.enter_context(nc.sbuf_tensor("wdb", [16, 80, 128], I16))
        blk = ps.enter_context(nc.Block())
        c.barrier(blk)

        @blk.vector
        def _(v):
            v = W(v)
            v.memset(zi16[:, :], 0)
            inc("s_vec", v.nop(), 1)

        m_z = n_of("s_vec")
        st2_b = n_of("s_st2")

        @blk.sync
        def _(sy):
            sy.wait_ge(c.sem["s_vec"], m_z)
            tz = c.tbl_hbm.ap().rearrange("(a p r) k -> a p (r k)", a=8, p=128)
            for a in range(8):
                inc("s_st2", sy.dma_start(tz[a, :, :], zi16[:, :]), 16)

        gp_b = n_of("s_gp")

        @blk.gpsimd
        def _(g):
            inc("s_gp", g.dma_start(tokid_s[:, :, :], c.tokid_h[:, :, :]), 16)
            g.wait_ge(c.sem["s_gp"], n_of("s_gp"))
            g.wait_ge(c.sem["s_st2"], st2_b + 8 * 16)
            inc("s_gp", g.dma_scatter_add(c.tbl_hbm[:, :], tokid_s[:, :, :], c.w0w[:, :],
                                          num_idxs=L, num_idxs_reg=L, elem_size=128), 16)
            g.wait_ge(c.sem["s_gp"], n_of("s_gp"))
            inc("s_gp", g.dma_scatter_add(c.tbl_hbm[:, :], tokid_s[:, :, :], c.w1w[:, :],
                                          num_idxs=L, num_idxs_reg=L, elem_size=128), 16)
            g.wait_ge(c.sem["s_gp"], n_of("s_gp"))
            inc("s_gp3", g.nop(), 1)

        gp3_t = n_of("s_gp3")
        vecb = n_of("s_vec")

        @blk.sync
        def _(sy):
            sy.wait_ge(c.sem["s_gp3"], gp3_t)
            for a in range(8):
                if a >= 1:
                    sy.wait_ge(c.sem["s_vec"], vecb + a)
                srcp = bass.AP(c.tbl_hbm, a * 1280 * 128,
                               [[128, 16], [16 * 128, 80], [1, 128]])
                inc("s_st2", sy.dma_start(wdb[:, :, :], srcp), 16)
                sy.wait_ge(c.sem["s_st2"], n_of("s_st2"))
                inc("s_gp3", sy.nop(), 1)
            sy.wait_ge(c.sem["s_vec"], vecb + 8)
            inc("s_st2", sy.dma_start(c.wdisp_hbm[0:16, :], c.wdisp[0:16, :]), 16)
            inc("s_st2", sy.dma_start(c.wdisp_hbm[16:32, :], c.wdisp[0:16, :]), 16)
            sy.wait_ge(c.sem["s_st2"], n_of("s_st2"))
            inc("s_st2", sy.dma_start(c.wdisp[0:32, :], c.wdisp_hbm[:, :]), 16)
            sy.wait_ge(c.sem["s_st2"], n_of("s_st2"))
            inc("s_gp3", sy.nop(), 1)

        @blk.vector
        def _(v):
            v = W(v)
            for a in range(8):
                v.wait_ge(c.sem["s_gp3"], gp3_t + a + 1)
                srcw = bass.AP(wdb, 0, [[80 * 128, 16], [128, 80]])
                inc("s_vec", v.tensor_copy(
                    c.wdisp[0:16, a * 80:(a + 1) * 80], srcw), 1)

        if "wdisp" in c.tap_h or "wrow" in c.tap_h:
            m_wd = n_of("s_vec")

            @blk.sync
            def _(sy):
                sy.wait_ge(c.sem["s_vec"], m_wd)
                if "wdisp" in c.tap_h:
                    inc("s_st2", sy.dma_start(c.tap_h["wdisp"][:, :], c.wdisp[0:32, :]), 16)
                if "wrow" in c.tap_h:
                    inc("s_st2", sy.dma_start(c.tap_h["wrow"][:, :], c.wrow_hbm[:, :]), 16)
                sy.wait_ge(c.sem["s_st2"], n_of("s_st2"))


def _p3(c):
    nc, inc, n_of = c.nc, c.inc, c.n_of
    with ExitStack() as ps:
        wvsb = ps.enter_context(nc.sbuf_tensor("wvsb", [128, NC, D], BF16))
        vcwsb = ps.enter_context(nc.sbuf_tensor("vcwsb", [128, NC, 4], F32))
        vpre = ps.enter_context(nc.sbuf_tensor("vpre", [128, 3 + RG], F32))
        carry = ps.enter_context(nc.sbuf_tensor("carry3", [128, NC, 3], F32))
        cva = ps.enter_context(nc.sbuf_tensor("cva", [128, RG], F32))
        cvb = ps.enter_context(nc.sbuf_tensor("cvb", [128, RG], F32))
        vgrp = ps.enter_context(nc.sbuf_tensor("vgrp", [128, NC, RG], BF16))
        zpad = ps.enter_context(nc.sbuf_tensor("zpad", [128, NC, 32], BF16))
        pv3 = ps.enter_context(nc.psum_tensor("pv3", [128, 2, 512], F32))
        blk = ps.enter_context(nc.Block())
        c.barrier(blk)
        NTT = RG // 512
        vh = c.v_hbm.ap().rearrange("(kc p) m -> p kc m", p=128)

        @blk.gpsimd
        def _(g):
            inc("s_w", g.dma_start(
                wvsb[:, :, :], c.wv_h.ap().rearrange("(kc p) m -> p kc m", p=128)), 16)
            inc("s_w", g.dma_start(vcwsb[:, :, :], c.vcw_h[:, :, :]), 16)
            g.wait_ge(c.sem["s_w"], n_of("s_w"))
            inc("s_3w", g.nop(), 1)

        gpb = n_of("s_3w")

        @blk.vector
        def _(v):
            v = W(v)
            v.memset(carry[:, :, :], 0.0)
            v.memset(zpad[:, :, :], 0.0)
            inc("s_vec", v.nop(), 1)
            for g in range(NRG):
                for oc in range(NC):
                    i = g * NC + oc
                    v.wait_ge(c.sem["s_3ev"], i + 1)
                    if i >= 1:
                        v.wait_ge(c.sem["s_3si"], i)
                    v.tensor_scalar(cva[:, :], vpre[:, 3:3 + RG], vcwsb[:, oc, 3:4],
                                    None, ALU.mult)
                    v.scalar_tensor_tensor(cvb[:, :], vpre[:, 2:2 + RG], vcwsb[:, oc, 2:3],
                                           cva[:, :], ALU.mult, ALU.add)
                    v.scalar_tensor_tensor(cva[:, :], vpre[:, 1:1 + RG], vcwsb[:, oc, 1:2],
                                           cvb[:, :], ALU.mult, ALU.add)
                    v.scalar_tensor_tensor(cvb[:, :], vpre[:, 0:RG], vcwsb[:, oc, 0:1],
                                           cva[:, :], ALU.mult, ALU.add)
                    v.tensor_copy(carry[:, oc, :], vpre[:, RG:RG + 3])
                    inc("s_3cv", v.nop(), 1)

        vecb = n_of("s_vec")

        @blk.tensor
        def _(te):
            te.wait_ge(c.sem["s_3w"], gpb)
            for g in range(NRG):
                for oc in range(NC):
                    i = g * NC + oc
                    for tt in range(NTT):
                        if i >= 1 or tt >= 1:
                            pass
                        if i >= 1:
                            te.wait_ge(c.sem["s_3ev"], i)
                        for kc in range(NC):
                            ins = te.matmul(
                                pv3[:, tt, :], wvsb[:, kc, oc * 128:(oc + 1) * 128],
                                c.xn_fm[:, kc, g * RG + tt * 512:g * RG + (tt + 1) * 512],
                                start=(kc == 0), stop=(kc == NC - 1))
                        inc("s_3mm", ins, 1)

        @blk.scalar
        def _(sc):
            sc = W(sc)
            sc.wait_ge(c.sem["s_vec"], vecb)
            gp3b = n_of("s_gp3")
            for g in range(NRG):
                for oc in range(NC):
                    i = g * NC + oc
                    if oc == 0 and g >= 1:
                        sc.wait_ge(c.sem["s_gp3"], gp3b + g)
                    sc.copy(vpre[:, 0:3], carry[:, oc, :])
                    for tt in range(NTT):
                        sc.wait_ge(c.sem["s_3mm"], i * NTT + tt + 1)
                        sc.copy(vpre[:, 3 + tt * 512:3 + (tt + 1) * 512], pv3[:, tt, :])
                    inc("s_3ev", sc.nop(), 1)
                    sc.wait_ge(c.sem["s_3cv"], i + 1)
                    if g >= 1 and False:
                        pass
                    inc("s_3si", sc.activation(vgrp[:, oc, :], cvb[:, :], AF.Silu), 1)

        @blk.sync
        def _(sy):
            sy.wait_ge(c.sem["s_vec"], vecb)
            inc("s_3st", sy.dma_start(vh[:, :, 0:32], zpad[:, :, :]), 16)
            for g in range(NRG):
                sy.wait_ge(c.sem["s_3si"], (g + 1) * NC)
                inc("s_3st", sy.dma_start(vh[:, :, 32 + g * RG:32 + (g + 1) * RG],
                                          vgrp[:, :, :]), 16)
                sy.wait_ge(c.sem["s_3st"], n_of("s_3st"))
                inc("s_gp3", sy.nop(), 1)

    return


def _p35(c):
    nc, inc, n_of = c.nc, c.inc, c.n_of
    NPK = DG // 128
    with ExitStack() as ps:
        wqsb = ps.enter_context(nc.sbuf_tensor("wqsb", [128, NC, D], BF16))
        wksb = ps.enter_context(nc.sbuf_tensor("wksb", [128, NC, D], BF16))
        wbsb = ps.enter_context(nc.sbuf_tensor("wbsb", [128, NC, H], BF16))
        qcwsb = ps.enter_context(nc.sbuf_tensor("qcwsb", [128, NC, 4], F32))
        kcwsb = ps.enter_context(nc.sbuf_tensor("kcwsb", [128, NC, 4], F32))
        pre5 = ps.enter_context(nc.sbuf_tensor("pre5", [128, 3 + DG], F32))
        cqa = ps.enter_context(nc.sbuf_tensor("cqa", [128, DG], F32))
        cqb = ps.enter_context(nc.sbuf_tensor("cqb", [128, DG], F32))
        carq = ps.enter_context(nc.sbuf_tensor("carq", [128, NC, 3], F32))
        cark = ps.enter_context(nc.sbuf_tensor("cark", [128, NC, 3], F32))
        qs5 = ps.enter_context(nc.sbuf_tensor("qs5", [128, NC, DG], BF16))
        qf5 = ps.enter_context(nc.sbuf_tensor("qf5", [128, NC, DG], BF16))
        kf5 = ps.enter_context(nc.sbuf_tensor("kf5", [128, NC, DG], BF16))
        kb5 = ps.enter_context(nc.sbuf_tensor("kb5", [128, NC, DG], BF16))
        vb5 = ps.enter_context(nc.sbuf_tensor("vb5", [128, NC, DG], BF16))
        vg5 = ps.enter_context(nc.sbuf_tensor("vg5", [128, NC, DG], BF16))
        sqb = ps.enter_context(nc.sbuf_tensor("sqb5", [128, DG], BF16))
        t4 = ps.enter_context(nc.sbuf_tensor("t45", [4, DG], F32))
        r4 = ps.enter_context(nc.sbuf_tensor("r45", [4, DG], F32))
        rqb4 = ps.enter_context(nc.sbuf_tensor("rqb45", [4, DG], BF16))
        bb4 = ps.enter_context(nc.sbuf_tensor("bb45", [4, DG], BF16))
        kcm = ps.enter_context(nc.sbuf_tensor("kcm", [128, NPK, 1024], BF16))
        kbcm = ps.enter_context(nc.sbuf_tensor("kbcm", [128, NPK, 1024], BF16))
        vbcm = ps.enter_context(nc.sbuf_tensor("vbcm", [128, NPK, 1024], BF16))
        tsb = ps.enter_context(nc.sbuf_tensor("tsb5", [128, 128], BF16))
        tt5 = ps.enter_context(nc.sbuf_tensor("tt5", [128, 128], BF16))
        xs5 = ps.enter_context(nc.sbuf_tensor("xs5", [128, 128], BF16))
        xt5 = ps.enter_context(nc.sbuf_tensor("xt5", [128, 128], BF16))
        zz5 = ps.enter_context(nc.sbuf_tensor("zz5", [128, 128], BF16))
        tf5 = ps.enter_context(nc.sbuf_tensor("tf5", [128, 128], F32))
        inv5 = ps.enter_context(nc.sbuf_tensor("inv5", [128, 128], BF16))
        atm5 = ps.enter_context(nc.sbuf_tensor("atm5", [128, 128], BF16))
        att5 = ps.enter_context(nc.sbuf_tensor("att5", [128, 128], BF16))
        utm5 = ps.enter_context(nc.sbuf_tensor("utm5", [128, 1024], BF16))
        wsb5 = ps.enter_context(nc.sbuf_tensor("wsb5", [128, 1024], BF16))
        uadj5 = ps.enter_context(nc.sbuf_tensor("uadj5", [128, 1024], BF16))
        ssb5 = ps.enter_context(nc.sbuf_tensor("ssb5", [128, 2048], BF16))
        dtm5 = ps.enter_context(nc.sbuf_tensor("dtm5", [128, NPK, 1024], BF16))
        dfm5 = ps.enter_context(nc.sbuf_tensor("dfm5", [128, NC, DG], BF16))
        pm0 = ps.enter_context(nc.psum_tensor("pm0", [128, 512], F32))
        pbig = ps.enter_context(nc.psum_tensor("pbig5", [128, 1024], F32))
        pcmT = ps.enter_context(nc.psum_tensor("pcmT", [128, 1024], BF16))
        pS = ps.enter_context(nc.psum_tensor("pS5", [128, 2048], F32))
        blk = ps.enter_context(nc.Block())
        c.barrier(blk)

        b_a = n_of("s_5a")
        b_b = n_of("s_5b")
        b_c = n_of("s_5c")
        b_d = n_of("s_5d")
        b_e = n_of("s_5e")
        ctr = {"a": b_a, "b": b_b, "c": b_c, "d": b_d, "e": b_e}

        def mark(which):
            ctr[which] += 1
            return ctr[which]

        vh = c.v_hbm.ap().rearrange("(kc p) m -> p kc m", p=128)
        dh = c.delta_hbm.ap().rearrange("(kc p) m -> p kc m", p=128)

        @blk.gpsimd
        def _(g):
            inc("s_w", g.dma_start(
                wqsb[:, :, :], c.wq_h.ap().rearrange("(kc p) m -> p kc m", p=128)), 16)
            inc("s_w", g.dma_start(
                wksb[:, :, :], c.wk_h.ap().rearrange("(kc p) m -> p kc m", p=128)), 16)
            inc("s_w", g.dma_start(
                wbsb[:, :, :], c.wb_h.ap().rearrange("(kc p) m -> p kc m", p=128)), 16)
            inc("s_w", g.dma_start(qcwsb[:, :, :], c.qcw_h[:, :, :]), 16)
            inc("s_w", g.dma_start(kcwsb[:, :, :], c.kcw_h[:, :, :]), 16)
            g.wait_ge(c.sem["s_w"], n_of("s_w"))
            inc("s_5e", g.nop(), 1)

        e_w = ctr["e"] = n_of("s_5e")

        A = {"a": [], "b": [], "c": [], "d": []}

        def TE(f):
            A["a"].append(f)

        def VE(f):
            A["b"].append(f)

        def SC(f):
            A["c"].append(f)

        def SY(f):
            A["d"].append(f)

        sem_of = {"a": "s_5a", "b": "s_5b", "c": "s_5c", "d": "s_5d"}

        def w(eng, which, val):
            A[eng].append(lambda e, _s=sem_of[which], _v=val: e.wait_ge(c.sem[_s], _v))

        def m(eng):
            which = {"a": "a", "b": "b", "c": "c", "d": "d"}[eng]
            val = mark(which)
            A[eng].append(lambda e, _s=sem_of[which]: inc(_s, e.nop(), 1))
            return val

        def mi(eng, emitf):
            which = eng
            val = mark(which)
            def g(e, _f=emitf, _s=sem_of[which]):
                inc(_s, _f(e), 1)
            A[eng].append(g)
            return val

        for g in range(NDG):
            g0 = g * DG
            dv = mi("d", lambda e, g=g: e.dma_start(
                vg5[:, :, :], vh[:, :, 32 + g * DG:32 + (g + 1) * DG]))
            pass
        A = {"a": [], "b": [], "c": [], "d": []}
        ctr["a"], ctr["b"], ctr["c"], ctr["d"] = b_a, b_b, b_c, b_d

        dmac = {"n": 0}

        def SYD(dst, srcap):
            def f(e, _d=dst, _s=srcap):
                inc("s_st2", e.dma_start(_d, _s), 16)
                e.wait_ge(c.sem["s_st2"], n_of("s_st2"))
            A["d"].append(f)

        GL = c.glim
        for g in range(GL):
            g0 = g * DG

            SYD(vg5[:, :, :], vh[:, :, 32 + g0:32 + g0 + DG])
            vld_m = m("d")

            for qk in range(2):
                wsb = wqsb if qk == 0 else wksb
                cwsb = qcwsb if qk == 0 else kcwsb
                car = carq if qk == 0 else cark
                outsil = qs5
                for oc in range(NC):
                    def f(e, oc=oc, g0=g0, wsb=wsb):
                        for kc in range(NC):
                            ins = e.matmul(pbig[:, 0:DG], wsb[:, kc, oc * 128:(oc + 1) * 128],
                                           c.xn_fm[:, kc, g0:g0 + DG],
                                           start=(kc == 0), stop=(kc == NC - 1))
                        inc("s_5a", ins, 1)
                    A["a"].append(f); amk = mark("a")
                    def f(e, oc=oc, amk=amk, car=car):
                        e.wait_ge(c.sem["s_5a"], amk)
                        e.copy(pre5[:, 0:3], car[:, oc, :])
                        inc("s_5c", e.copy(pre5[:, 3:3 + DG], pbig[:, 0:DG]), 1)
                    A["c"].append(f); cmk = mark("c")
                    def f(e, oc=oc, cmk=cmk, cwsb=cwsb, car=car):
                        e.wait_ge(c.sem["s_5c"], cmk)
                        e.tensor_scalar(cqa[:, :], pre5[:, 3:3 + DG], cwsb[:, oc, 3:4],
                                        None, ALU.mult)
                        e.scalar_tensor_tensor(cqb[:, :], pre5[:, 2:2 + DG], cwsb[:, oc, 2:3],
                                               cqa[:, :], ALU.mult, ALU.add)
                        e.scalar_tensor_tensor(cqa[:, :], pre5[:, 1:1 + DG], cwsb[:, oc, 1:2],
                                               cqb[:, :], ALU.mult, ALU.add)
                        e.scalar_tensor_tensor(cqb[:, :], pre5[:, 0:DG], cwsb[:, oc, 0:1],
                                               cqa[:, :], ALU.mult, ALU.add)
                        inc("s_5b", e.tensor_copy(car[:, oc, :], pre5[:, DG:DG + 3]), 1)
                    A["b"].append(f); bmk = mark("b")
                    def f(e, oc=oc, bmk=bmk):
                        e.wait_ge(c.sem["s_5b"], bmk)
                        inc("s_5c", e.activation(qs5[:, oc, :], cqb[:, :], AF.Silu), 1)
                    A["c"].append(f); cmk2 = mark("c")
                    def f(e, oc=oc, cmk2=cmk2):
                        e.wait_ge(c.sem["s_5c"], cmk2)
                        inc("s_5b", e.tensor_tensor(sqb[:, :], qs5[:, oc, :], qs5[:, oc, :],
                                                    ALU.mult), 1)
                    A["b"].append(f); bmk2 = mark("b")
                    def f(e, oc=oc, bmk2=bmk2):
                        e.wait_ge(c.sem["s_5b"], bmk2)
                        h = oc // 2
                        ins = e.matmul(pm0[0:4, 0:DG], c.colsel[:, 4 * h:4 * h + 4],
                                       sqb[:, :], start=(oc == 0), stop=(oc == NC - 1))
                        inc("s_5a", ins, 1)
                    A["a"].append(f); amk2 = mark("a")
                def f(e, amk2=amk2):
                    e.wait_ge(c.sem["s_5a"], amk2)
                    e.tensor_scalar(t4[:, :], pm0[0:4, 0:DG], 1e-6, None, ALU.add)
                    inc("s_5b", e.reciprocal(r4[:, :], t4[:, :]), 1)
                A["b"].append(f); bmk3 = mark("b")
                def f(e, bmk3=bmk3):
                    e.wait_ge(c.sem["s_5b"], bmk3)
                    inc("s_5c", e.activation(rqb4[:, :], r4[:, :], AF.Sqrt), 1)
                A["c"].append(f); cmk3 = mark("c")
                for h in range(H):
                    def f(e, h=h, cmk3=cmk3):
                        e.wait_ge(c.sem["s_5c"], cmk3)
                        ins = e.matmul(pm0[:, 256:256 + DG], c.bcast4[:, 128 * h:128 * (h + 1)],
                                       rqb4[:, :], start=True, stop=True)
                        inc("s_5a", ins, 1)
                    A["a"].append(f); amk3 = mark("a")
                    def f(e, h=h, amk3=amk3, qk=qk):
                        e.wait_ge(c.sem["s_5a"], amk3)
                        dst = qf5 if qk == 0 else kf5
                        e.tensor_tensor(dst[:, 2 * h, :], qs5[:, 2 * h, :],
                                        pm0[:, 256:256 + DG], ALU.mult)
                        inc("s_5b", e.tensor_tensor(dst[:, 2 * h + 1, :], qs5[:, 2 * h + 1, :],
                                                    pm0[:, 256:256 + DG], ALU.mult), 1)
                    A["b"].append(f); bmk4 = mark("b")
                    w("a", "b", bmk4)

            def f(e, g0=g0):
                for kc in range(NC):
                    ins = e.matmul(pm0[0:4, 0:DG], wbsb[:, kc, :],
                                   c.xn_fm[:, kc, g0:g0 + DG],
                                   start=(kc == 0), stop=(kc == NC - 1))
                inc("s_5a", ins, 1)
            A["a"].append(f); amk4 = mark("a")
            def f(e, amk4=amk4):
                e.wait_ge(c.sem["s_5a"], amk4)
                inc("s_5c", e.activation(bb4[:, :], pm0[0:4, 0:DG], AF.Sigmoid), 1)
            A["c"].append(f); cmk4 = mark("c")
            for h in range(H):
                def f(e, h=h, cmk4=cmk4, vld_m=vld_m):
                    e.wait_ge(c.sem["s_5c"], cmk4)
                    e.wait_ge(c.sem["s_5d"], vld_m)
                    ins = e.matmul(pm0[:, 256:256 + DG], c.bcast4[:, 128 * h:128 * (h + 1)],
                                   bb4[:, :], start=True, stop=True)
                    inc("s_5a", ins, 1)
                A["a"].append(f); amk5 = mark("a")
                def f(e, h=h, amk5=amk5):
                    e.wait_ge(c.sem["s_5a"], amk5)
                    e.tensor_tensor(kb5[:, 2 * h, :], kf5[:, 2 * h, :],
                                    pm0[:, 256:256 + DG], ALU.mult)
                    e.tensor_tensor(kb5[:, 2 * h + 1, :], kf5[:, 2 * h + 1, :],
                                    pm0[:, 256:256 + DG], ALU.mult)
                    e.tensor_tensor(vb5[:, 2 * h, :], vg5[:, 2 * h, :],
                                    pm0[:, 256:256 + DG], ALU.mult)
                    inc("s_5b", e.tensor_tensor(vb5[:, 2 * h + 1, :], vg5[:, 2 * h + 1, :],
                                                pm0[:, 256:256 + DG], ALU.mult), 1)
                A["b"].append(f); bmk5 = mark("b")
                w("a", "b", bmk5)

            if c.slim < 2:
                continue
            for ti, (srct, dstt) in enumerate([(kf5, kcm), (kb5, kbcm), (vb5, vbcm)]):
                for p in range(NPK):
                    def f(e, srct=srct, p=p):
                        for r in range(4):
                            for oc in range(NC):
                                ins = e.transpose(
                                    pcmT[32 * r:32 * r + 32, oc * 128:(oc + 1) * 128],
                                    srct[:, oc, (p * 4 + r) * 32:(p * 4 + r) * 32 + 32],
                                    c.identb[:, :], tile_position=(0, 32 * r))
                        inc("s_5a", ins, 1)
                    A["a"].append(f); amk6 = mark("a")
                    def f(e, dstt=dstt, p=p, amk6=amk6):
                        e.wait_ge(c.sem["s_5a"], amk6)
                        inc("s_5b", e.tensor_copy(dstt[:, p, :], pcmT[:, :]), 1)
                    A["b"].append(f); bmk6 = mark("b")
                    w("a", "b", bmk6)

            if c.slim < 3:
                continue
            for p in range(NPK):
                pc0 = g0 + p * 128
                def f(e, p=p, g0=g0):
                    for r in range(4):
                        for h in range(H):
                            for dc in range(2):
                                oc = 2 * h + dc
                                cs = (p * 4 + r) * 32
                                ins = e.matmul(
                                    pm0[32 * r:32 * r + 32, 32 * h:32 * h + 32],
                                    kb5[:, oc, cs:cs + 32], kf5[:, oc, cs:cs + 32],
                                    start=(h == 0 and dc == 0),
                                    stop=(h == H - 1 and dc == 1),
                                    tile_position=(0, 32 * r))
                    inc("s_5a", ins, 1)
                A["a"].append(f); amk7 = mark("a")
                def f(e, amk7=amk7):
                    e.wait_ge(c.sem["s_5a"], amk7)
                    e.tensor_tensor(tf5[:, :], pm0[:, 0:128], c.slow128[:, :], ALU.mult)
                    e.tensor_tensor(tsb[:, :], tf5[:, :], c.eyeb128[:, :], ALU.add)
                    e.transpose(tt5[:, :], tsb[:, :])
                    e.tensor_tensor(xs5[:, :], c.eyeb2[:, :], tsb[:, :], ALU.subtract)
                    inc("s_5b", e.nop(), 1)
                A["b"].append(f); bmk7 = mark("b")
                last_b = bmk7
                for it in range(4):
                    def f(e, last_b=last_b):
                        e.wait_ge(c.sem["s_5b"], last_b)
                        for r in range(4):
                            for h in range(H):
                                ins = e.matmul(
                                    pm0[32 * r:32 * r + 32, 128 + 32 * h:128 + 32 * h + 32],
                                    tt5[32 * r:32 * r + 32, 32 * h:32 * h + 32],
                                    xs5[32 * r:32 * r + 32, 32 * h:32 * h + 32],
                                    start=(h == 0), stop=(h == H - 1),
                                    tile_position=(32 * r, 32 * r))
                        inc("s_5a", ins, 1)
                    A["a"].append(f); amk8 = mark("a")
                    def f(e, amk8=amk8):
                        e.wait_ge(c.sem["s_5a"], amk8)
                        e.tensor_tensor(zz5[:, :], c.eyeb2[:, :], pm0[:, 128:256], ALU.subtract)
                        e.transpose(xt5[:, :], xs5[:, :])
                        inc("s_5b", e.nop(), 1)
                    A["b"].append(f); bmk8 = mark("b")
                    def f(e, bmk8=bmk8):
                        e.wait_ge(c.sem["s_5b"], bmk8)
                        for r in range(4):
                            for h in range(H):
                                ins = e.matmul(
                                    pm0[32 * r:32 * r + 32, 256 + 32 * h:256 + 32 * h + 32],
                                    xt5[32 * r:32 * r + 32, 32 * h:32 * h + 32],
                                    zz5[32 * r:32 * r + 32, 32 * h:32 * h + 32],
                                    start=(h == 0), stop=(h == H - 1),
                                    tile_position=(32 * r, 32 * r))
                        inc("s_5a", ins, 1)
                    A["a"].append(f); amk9 = mark("a")
                    def f(e, amk9=amk9):
                        e.wait_ge(c.sem["s_5a"], amk9)
                        inc("s_5b", e.tensor_copy(xs5[:, :], pm0[:, 256:384]), 1)
                    A["b"].append(f); last_b = mark("b")
                def f(e, last_b=last_b):
                    e.wait_ge(c.sem["s_5b"], last_b)
                    inc("s_5b", e.transpose(inv5[:, :], xs5[:, :]), 1)
                A["b"].append(f); bmk9 = mark("b")
                def f(e, p=p, bmk9=bmk9):
                    e.wait_ge(c.sem["s_5b"], bmk9)
                    for r in range(4):
                        for h in range(H):
                            ins = e.matmul(
                                pbig[32 * r:32 * r + 32, 256 * h:256 * (h + 1)],
                                inv5[32 * r:32 * r + 32, 32 * h:32 * h + 32],
                                vbcm[32 * r:32 * r + 32, p, 256 * h:256 * (h + 1)],
                                start=(h in (0, 2)), stop=(h in (1, 3)),
                                tile_position=(32 * r, 32 * r))
                    inc("s_5a", ins, 1)
                A["a"].append(f); amk10 = mark("a")
                def f(e, amk10=amk10):
                    e.wait_ge(c.sem["s_5a"], amk10)
                    inc("s_5b", e.tensor_copy(utm5[:, :], pbig[:, :]), 1)
                A["b"].append(f); bmk10 = mark("b")
                def f(e, p=p, bmk10=bmk10):
                    e.wait_ge(c.sem["s_5b"], bmk10)
                    for r in range(4):
                        for h in range(H):
                            for dc in range(2):
                                oc = 2 * h + dc
                                ins = e.matmul(
                                    pbig[:, oc * 128 + 32 * r:oc * 128 + 32 * r + 32],
                                    kbcm[32 * r:32 * r + 32, p, oc * 128:(oc + 1) * 128],
                                    inv5[32 * r:32 * r + 32, 32 * h:32 * h + 32],
                                    start=(r == 0 and dc == 0 and h in (0, 2)),
                                    stop=(r == 3 and dc == 1 and h in (1, 3)),
                                    tile_position=(32 * r, 0))
                    inc("s_5a", ins, 1)
                A["a"].append(f); amk11 = mark("a")
                def f(e, amk11=amk11):
                    e.wait_ge(c.sem["s_5a"], amk11)
                    inc("s_5b", e.tensor_copy(wsb5[:, :], pbig[:, :]), 1)
                A["b"].append(f); bmk11 = mark("b")
                def f(e, p=p, g0=g0, bmk11=bmk11):
                    e.wait_ge(c.sem["s_5b"], bmk11)
                    for r in range(4):
                        for h in range(H):
                            for dc in range(2):
                                oc = 2 * h + dc
                                cs = (p * 4 + r) * 32
                                ins = e.matmul(
                                    pm0[32 * r:32 * r + 32, 384 + 32 * h:384 + 32 * h + 32],
                                    qf5[:, oc, cs:cs + 32], kf5[:, oc, cs:cs + 32],
                                    start=(h == 0 and dc == 0),
                                    stop=(h == H - 1 and dc == 1),
                                    tile_position=(0, 32 * r))
                    inc("s_5a", ins, 1)
                A["a"].append(f); amk12 = mark("a")
                def f(e, amk12=amk12):
                    e.wait_ge(c.sem["s_5a"], amk12)
                    e.tensor_tensor(atm5[:, :], pm0[:, 384:512], c.lowi128[:, :], ALU.mult)
                    inc("s_5b", e.transpose(att5[:, :], atm5[:, :]), 1)
                A["b"].append(f); bmk12 = mark("b")
                w("a", "b", bmk12)

                for r in range(4):
                    ch = p * 4 + r
                    cs = g0 + ch * 32
                    def f(e, r=r):
                        for h in range(H):
                            for dc in range(2):
                                oc = 2 * h + dc
                                ins = e.matmul(
                                    pbig[32 * r:32 * r + 32, 256 * h:256 * (h + 1)],
                                    wsb5[:, oc * 128 + 32 * r:oc * 128 + 32 * r + 32],
                                    ssb5[:, 256 * oc:256 * (oc + 1)],
                                    start=(dc == 0 and h in (0, 2)),
                                    stop=(dc == 1 and h in (1, 3)),
                                    tile_position=(0, 32 * r))
                        inc("s_5a", ins, 1)
                    A["a"].append(f); amk13 = mark("a")
                    def f(e, r=r, amk13=amk13):
                        e.wait_ge(c.sem["s_5a"], amk13)
                        inc("s_5b", e.tensor_tensor(
                            uadj5[32 * r:32 * r + 32, :], utm5[32 * r:32 * r + 32, :],
                            pbig[32 * r:32 * r + 32, :], ALU.subtract), 1)
                    A["b"].append(f); bmk13 = mark("b")
                    def f(e, r=r, p=p, bmk13=bmk13, g=g, ch=ch):
                        e.wait_ge(c.sem["s_5b"], bmk13)
                        for h in range(H):
                            for dc in range(2):
                                oc = 2 * h + dc
                                cs2 = (p * 4 + r) * 32
                                e.matmul(
                                    pbig[32 * r:32 * r + 32, 256 * h:256 * (h + 1)],
                                    qf5[:, oc, cs2:cs2 + 32],
                                    ssb5[:, 256 * oc:256 * (oc + 1)],
                                    start=(dc == 0 and h in (0, 2)), stop=False,
                                    tile_position=(0, 32 * r))
                        for h in range(H):
                            e.matmul(
                                pbig[32 * r:32 * r + 32, 256 * h:256 * (h + 1)],
                                att5[32 * r:32 * r + 32, 32 * h:32 * h + 32],
                                uadj5[32 * r:32 * r + 32, 256 * h:256 * (h + 1)],
                                start=False, stop=(h in (1, 3)),
                                tile_position=(32 * r, 32 * r))
                        first_bank = (g == 0 and ch == 0)
                        last_bank = (g == GL - 1 and ch == 7)
                        for h in range(H):
                            for dc in range(2):
                                oc = 2 * h + dc
                                ins = e.matmul(
                                    pS[:, 256 * oc:256 * (oc + 1)],
                                    kcm[32 * r:32 * r + 32, p,
                                        oc * 128:(oc + 1) * 128],
                                    uadj5[32 * r:32 * r + 32, 256 * h:256 * (h + 1)],
                                    start=(first_bank and dc == 0),
                                    stop=(last_bank and dc == 1),
                                    skip_group_check=True,
                                    tile_position=(32 * r, 0))
                        inc("s_5a", ins, 1)
                    A["a"].append(f); amk14 = mark("a")
                    def f(e, r=r, p=p, amk14=amk14):
                        e.wait_ge(c.sem["s_5a"], amk14)
                        e.tensor_copy(dtm5[32 * r:32 * r + 32, p, :],
                                      pbig[32 * r:32 * r + 32, :])
                        inc("s_5b", e.tensor_copy(ssb5[:, :], pS[:, :]), 1)
                    A["b"].append(f); bmk14 = mark("b")
                    w("a", "b", bmk14)

            if c.slim < 4:
                continue
            for p in range(NPK):
                def f(e, p=p):
                    for oc in range(NC):
                        for q in range(4):
                            ins = e.transpose(
                                pcmT[32 * q:32 * q + 32, oc * 128:(oc + 1) * 128],
                                dtm5[:, p, (oc * 4 + q) * 32:(oc * 4 + q) * 32 + 32],
                                c.identb[:, :], tile_position=(0, 32 * q))
                    inc("s_5a", ins, 1)
                A["a"].append(f); amk15 = mark("a")
                def f(e, p=p, amk15=amk15):
                    e.wait_ge(c.sem["s_5a"], amk15)
                    for oc in range(NC):
                        ins = e.tensor_copy(dfm5[:, oc, p * 128:(p + 1) * 128],
                                            pcmT[:, oc * 128:(oc + 1) * 128])
                    inc("s_5b", ins, 1)
                A["b"].append(f); bmk15 = mark("b")
                w("a", "b", bmk15)
            def f(e, g0=g0, bmk15=bmk15):
                e.wait_ge(c.sem["s_5b"], bmk15)
                if c.slim >= 5:
                    inc("s_st2", e.dma_start(dh[:, :, g0:g0 + DG], dfm5[:, :, :]), 16)
                    e.wait_ge(c.sem["s_st2"], n_of("s_st2"))
            A["d"].append(f)
            dst_m = m("d")
            w("b", "d", dst_m)

        @blk.tensor
        def _(te):
            te.wait_ge(c.sem["s_5e"], e_w)
            te.wait_ge(c.sem["s_5f"], n_of("s_5f"))
            for f in A["a"]:
                f(te)

        @blk.vector
        def _(v):
            v = W(v)
            v.memset(carq[:, :, :], 0.0)
            v.memset(cark[:, :, :], 0.0)
            v.memset(ssb5[:, :], 0.0)
            inc("s_5f", v.nop(), 1)
            v.wait_ge(c.sem["s_5e"], e_w)
            for f in A["b"]:
                f(v)

        @blk.scalar
        def _(sc):
            sc = W(sc)
            sc.wait_ge(c.sem["s_5f"], n_of("s_5f"))
            for f in A["c"]:
                f(sc)

        @blk.sync
        def _(sy):
            for f in A["d"]:
                f(sy)


def _p4(c):
    nc, inc, n_of = c.nc, c.inc, c.n_of
    SBW = [256] * 5
    SBO = [0, 256, 512, 768, 1024]
    NSB = len(SBW)
    NST = CAPL // 128

    def ia_of(e, sb, jc):
        return (e * NSB + sb) * 32 + jc

    def ist_of(e, gst):
        return e * NST + gst

    with ExitStack() as ps:
        fc1sb = ps.enter_context(nc.sbuf_tensor("fc1sb", [128, 32, 8, 128], BF16))
        fc2sb = ps.enter_context(nc.sbuf_tensor("fc2sb", [128, 32, D], BF16))
        disp = ps.enter_context(nc.sbuf_tensor("disp", [128, 8, CAPL], BF16))
        disph = ps.enter_context(nc.sbuf_tensor("disph", [128, 5, D], BF16))
        h_sb = ps.enter_context(nc.sbuf_tensor("h_sb", [128, 32, 256], BF16))
        eosb = ps.enter_context(nc.sbuf_tensor("eosb", [128, 2, D], BF16))
        pa = ps.enter_context(nc.psum_tensor("pa4", [128, 2, 512], F32))
        peo = ps.enter_context(nc.psum_tensor("peo4", [128, 2, D], F32))
        ptr = ps.enter_context(nc.psum_tensor("ptr4", [128, 2, 8, 128], BF16))
        blk = ps.enter_context(nc.Block())
        c.barrier(blk)
        a0, h0, eo0, ev0, g0, w0, tp0, td0 = (n_of(s) for s in (
            "s_4a", "s_4h", "s_4eo", "s_4ev", "s_4g", "s_4w", "s_4tp", "s_4td"))
        stb = {0: n_of("s_4st"), 1: n_of("s_4s2")}

        @blk.gpsimd
        def _(g):
            for e in range(E):
                if e >= 1:
                    g.wait_ge(c.sem["s_4a"], a0 + e * NSB * 32)
                for hh in range(2):
                    g.wait_ge(c.sem["s_4tp"], tp0 + e * 10 + 5 * hh)
                    inc("s_4g", g.dma_gather(
                        disph[:, :, :], c.xn_hbm[:, :],
                        c.wdisp[:, e * 80 + 40 * hh:e * 80 + 40 * (hh + 1)],
                        num_idxs=640, num_idxs_reg=640, elem_size=D), 16)
                for q in range(4):
                    inc("s_4w", g.dma_start(
                        fc1sb[:, q * 8:(q + 1) * 8, :, :],
                        bass.AP(c.fc1t_h, (e * 32 + q * 8) * D * 128,
                                [[128, 128], [D * 128, 8], [128 * 128, 8], [1, 128]])), 16)
                if e >= 1:
                    g.wait_ge(c.sem["s_4eo"], eo0 + e * NST)
                inc("s_4w", g.dma_start(
                    fc2sb[:, :, :],
                    bass.AP(c.fc2_h, e * HID * D,
                            [[D, 128], [128 * D, 32], [1, D]])), 16)

        @blk.tensor
        def _(te):
            for e in range(E):
                for st in range(10):
                    te.wait_ge(c.sem["s_4g"], g0 + 32 * e + 16 * (st // 5 + 1))
                    if e * 10 + st >= 2:
                        te.wait_ge(c.sem["s_4td"], td0 + e * 10 + st - 1)
                    for dc in range(8):
                        ins = te.transpose(ptr[:, st % 2, dc, :],
                                           disph[:, st % 5, dc * 128:(dc + 1) * 128],
                                           c.identb[:, :])
                    inc("s_4tp", ins, 1)
                te.wait_ge(c.sem["s_4w"], w0 + 80 * (e + 1))
                for sb in range(NSB):
                    te.wait_ge(c.sem["s_4td"], td0 + e * 10 + 2 * (sb + 1))
                    for jc in range(32):
                        ia = ia_of(e, sb, jc)
                        if ia >= 2:
                            te.wait_ge(c.sem["s_4h"], h0 + ia - 1)
                        for dcc in range(8):
                            ins = te.matmul(
                                pa[:, jc % 2, 0:SBW[sb]],
                                fc1sb[:, jc, dcc, :],
                                disp[:, dcc, SBO[sb]:SBO[sb] + SBW[sb]],
                                start=(dcc == 0), stop=(dcc == 7))
                        inc("s_4a", ins, 1)
                    te.wait_ge(c.sem["s_4h"], h0 + ia_of(e, sb, 31) + 1)
                    for stl in range(SBW[sb] // 128):
                        gst = SBO[sb] // 128 + stl
                        ist = ist_of(e, gst)
                        if ist >= 2:
                            te.wait_ge(c.sem["s_4ev"], ev0 + ist - 1)
                        for jc in range(32):
                            for dh in range(2):
                                ins = te.matmul(
                                    peo[:, gst % 2, dh * 512:(dh + 1) * 512],
                                    h_sb[:, jc, stl * 128:(stl + 1) * 128],
                                    fc2sb[:, jc, dh * 512:(dh + 1) * 512],
                                    start=(jc == 0), stop=(jc == 31))
                        inc("s_4eo", ins, 1)

        @blk.scalar
        def _(sc):
            sc = W(sc)
            for e in range(E):
                for st in range(10):
                    sc.wait_ge(c.sem["s_4tp"], tp0 + e * 10 + st + 1)
                    if st == 0 and e >= 1:
                        sc.wait_ge(c.sem["s_4a"], a0 + e * NSB * 32)
                    inc("s_4td", sc.activation(
                        bass.AP(disp, st * 128, [[8 * CAPL, 128], [CAPL, 8], [1, 128]]),
                        ptr[:, st % 2, :, :], AF.Copy), 1)
                for sb in range(NSB):
                    for jc in range(32):
                        ia = ia_of(e, sb, jc)
                        sc.wait_ge(c.sem["s_4a"], a0 + ia + 1)
                        inc("s_4h", sc.activation(
                            h_sb[:, jc, 0:SBW[sb]], pa[:, jc % 2, 0:SBW[sb]],
                            AF.Relu, bias=c.fb1sb[:, e * 32 + jc:e * 32 + jc + 1]), 1)
                    for stl in range(SBW[sb] // 128):
                        gst = SBO[sb] // 128 + stl
                        ist = ist_of(e, gst)
                        sc.wait_ge(c.sem["s_4eo"], eo0 + ist + 1)
                        psem = "s_4st" if gst % 2 == 0 else "s_4s2"
                        if ist >= 2:
                            sc.wait_ge(c.sem[psem], n_of(psem))
                        inc("s_4ev", sc.activation(
                            eosb[:, gst % 2, :], peo[:, gst % 2, :], AF.Copy), 1)
                        inc(psem, sc.dma_start(
                            c.eo_hbm[e * CAPL + gst * 128:e * CAPL + (gst + 1) * 128, :],
                            eosb[:, gst % 2, :]), 16)

        @blk.sync
        def _(sy):
            sy.wait_ge(c.sem["s_4st"], n_of("s_4st"))
            sy.wait_ge(c.sem["s_4s2"], n_of("s_4s2"))


def _p5(c):
    nc, inc, n_of = c.nc, c.inc, c.n_of
    FIRC = {3: 0, 7: 3, 15: 10, 31: 25}
    seq5 = []
    for g5 in range(NRG):
        for jj5 in range(9):
            if jj5 < 8:
                for kc5 in range(NC):
                    seq5.append(("c", g5, jj5, kc5))
            if jj5 >= 1:
                seq5.append(("o", g5, jj5 - 1))
    pos5, slot5, cidx5 = {}, {}, {}
    ncu = 0
    for u5, it5 in enumerate(seq5):
        pos5[it5] = u5
        slot5[it5] = u5 % 2
        if it5[0] == "c":
            ncu += 1
            cidx5[it5] = ncu
    with ExitStack() as ps:
        sbl = lambda name, shape, dt: ps.enter_context(nc.sbuf_tensor(name, shape, dt))
        xng = sbl("xng", [128, NC, RG], BF16)
        vg = sbl("vg", [128, NC, 30 + RG], BF16)
        dg = sbl("dg5", [128, NC, RG], BF16)
        fir4 = sbl("fir4", [128, 4, NC, RG], BF16)
        w1sb = sbl("w1sb", [128, 2, 24, 128], BF16)
        wosb = sbl("wosb", [128, NC, D], BF16)
        w2sb = sbl("w2sb", [128, 16, 20], BF16)
        idwsb = sbl("idwsb", [128, NC, 4], BF16)
        hmidb = sbl("hmidb", [128, RG], BF16)
        dgsb = sbl("dgsb", [128, 2, 8, 128], BF16)
        firsb = sbl("firsb", [128, NC, 56], F32)
        cfa = sbl("cfa", [128, RG], F32)
        cfb = sbl("cfb", [128, RG], F32)
        rowsb = sbl("rowsb", [128, 2, 4, D], BF16)
        xres = sbl("xres", [128, D], F32)
        outsb = sbl("outsb", [128, D], F32)
        otF = sbl("otF", [128, NC, 128], F32)
        onf = sbl("onf", [128, NC, 128], BF16)
        lg5s = sbl("lg5s", [20, RG], F32)
        lgtm5 = sbl("lgtm5", [128, 8, 20], F32)
        ptm5 = sbl("ptm5", [128, 8, 20], F32)
        onwc = sbl("onwc", [128, 2], F32)
        b2sb = sbl("b2sb", [20, 1], F32)
        idbsb = sbl("idbsb", [4, 1], F32)
        ltausb = sbl("ltausb", [4, 1], F32)
        e5sb = sbl("e5sb", [4, 20], F32)
        taue = sbl("taue", [4, 1], F32)
        taum = sbl("taum", [4, 1], F32)
        taui = sbl("taui", [4, 1], F32)
        taur = sbl("taur", [1, 20], F32)
        onesr5 = sbl("onesr5", [1, 128], F32)
        tauix = sbl("tauix", [128, 20], F32)
        pvtm = sbl("pvtm", [128, 8, 4], F32)
        tfl = sbl("tfl", [128, 8, 4], F32)
        pvadj = sbl("pvadj", [128, 8, 4], F32)
        cmtm = sbl("cmtm", [128, 8, 4], F32)
        sft = sbl("sft", [128, 20], F32)
        sfe = sbl("sfe", [128, 20], F32)
        nm4 = sbl("nm4", [128, 4], F32)
        ssum = sbl("ssum", [128, 4], F32)
        sri = sbl("sri", [128, 4], F32)
        cmp5 = sbl("cmp5", [128, 20], F32)
        sumq = sbl("sumq", [128, 4], F32)
        sqb = sbl("sqb", [128, 4], F32)
        rq = sbl("rq", [128, 4], F32)
        rms = sbl("rms", [128, 4], F32)
        prout = ps.enter_context(nc.psum_tensor("prout", [128, 2, RG], F32))
        plg5 = ps.enter_context(nc.psum_tensor("plg5", [128, RG], F32))
        ptp = ps.enter_context(nc.psum_tensor("ptp", [128, 2, 8, 128], BF16))
        blk = ps.enter_context(nc.Block())
        c.barrier(blk)
        (ld0, fa0, bd0, fe0, r0, hm0, w10, fc0, gt0, pv0, tm0, lg0, sf0,
         on0, oe0, ob0, os0, xr0, wc0, cb0, w20) = (n_of(s) for s in (
             "s_5ld", "s_5fa", "s_5bd", "s_5fe", "s_5r", "s_5hm", "s_5w1",
             "s_5fc", "s_5gt", "s_5pv", "s_5tm", "s_5lg", "s_5sf", "s_5on",
             "s_5oe", "s_5ob", "s_5os", "s_5xr", "s_5wc", "s_5cb", "s_5w2"))
        g20 = n_of("s_5g2")
        cv0 = n_of("s_5cv")

        def slot_free_wait(eng, key):
            pu = pos5[key]
            if pu < 2:
                return
            pk = seq5[pu - 2]
            if pk[0] == "c":
                eng.wait_ge(c.sem["s_5cv"], cv0 + cidx5[pk])
            else:
                eng.wait_ge(c.sem["s_5oe"], oe0 + 8 * pk[1] + pk[2] + 1)

        def rin(kc, sl):
            if kc < 8:
                return xng[:, kc, sl]
            if kc < 40:
                return fir4[:, (kc - 8) // 8, kc % 8, sl]
            return dg[:, kc - 40, sl]

        @blk.sync
        def _(sy):
            for src, dst in [(c.fir_h, firsb), (c.onw_h, onwc), (c.b2_h, b2sb),
                             (c.idb_h, idbsb), (c.ltau_h, ltausb), (c.e5_h, e5sb)]:
                inc("s_5wc", sy.dma_start(dst[:], src[:]), 16)
            vhr = c.v_hbm.ap().rearrange("(kc p) m -> p kc m", p=128)
            dhr = c.delta_hbm.ap().rearrange("(kc p) m -> p kc m", p=128)
            xhr = c.xnf_hbm.ap().rearrange("(kc p) m -> p kc m", p=128)
            for g in range(NRG):
                if g >= 1:
                    sy.wait_ge(c.sem["s_5hm"], hm0 + 16 * g)
                inc("s_5ld", sy.dma_start(xng[:, :, :], xhr[:, :, g * RG:(g + 1) * RG]), 16)
                if g >= 1:
                    sy.wait_ge(c.sem["s_5cb"], cb0 + 64 * g)
                inc("s_5ld", sy.dma_start(
                    vg[:, :, :], vhr[:, :, 2 + g * RG:32 + (g + 1) * RG]), 16)
                inc("s_5ld", sy.dma_start(dg[:, :, :], dhr[:, :, g * RG:(g + 1) * RG]), 16)
                for oc in range(16):
                    gb = 16 * g + oc
                    for hb in range(2):
                        if gb >= 1:
                            pb = gb - 1
                            sy.wait_ge(c.sem["s_5r"],
                                       r0 + 56 * (pb // 16) + 3 * (pb % 16) + 1 + hb)
                        inc("s_5w1" if hb == 0 else "s_5w2", sy.dma_start(
                            w1sb[:, hb, :, :],
                            bass.AP(c.w1bf_hbm, oc * 128 + hb * 24 * 128 * 2048,
                                    [[2048, 128], [128 * 2048, 24], [1, 128]])), 16)
                for j in range(8):
                    gt = 8 * g + j
                    if gt >= 1:
                        sy.wait_ge(c.sem["s_5ob"], ob0 + gt)
                    inc("s_5xr", sy.dma_start(
                        xres[:, :],
                        bass.AP(c.x_h, gt * 128 * D, [[D, 128], [1, D]])), 16)

        @blk.gpsimd
        def _(g):
            inc("s_5gw", g.dma_start(
                wosb[:, :, :], c.wo_h.ap().rearrange("(kc p) m -> p kc m", p=128)), 16)
            inc("s_5gw", g.dma_start(
                w2sb[:, :, :],
                bass.AP(c.w2_h, 0, [[20, 128], [128 * 20, 16], [1, 20]])), 16)
            inc("s_5gw", g.dma_start(
                idwsb[:, :, :],
                bass.AP(c.idw_h, 0, [[4, 128], [128 * 4, 8], [1, 4]])), 16)
            for Q in range(NT // 2):
                if Q >= 2:
                    g.wait_ge(c.sem["s_5ob"], ob0 + 2 * (Q - 1))
                qsem = "s_5gt" if Q % 2 == 0 else "s_5g2"
                inc(qsem, g.dma_gather(
                    rowsb[:, Q % 2, :, :], c.eo_hbm[:, :],
                    c.wboth[:, Q * 32:(Q + 1) * 32],
                    num_idxs=512, num_idxs_reg=512, elem_size=D), 16)

        wc_all = n_of("s_5wc")
        gw_all = n_of("s_5gw")

        @blk.tensor
        def _(te):
            te.wait_ge(c.sem["s_5wc"], wc_all)
            te.wait_ge(c.sem["s_5gw"], gw_all)
            te.wait_ge(c.sem["s_5sf"], sf0 + 1)
            inc("s_5pv", te.matmul(plg5[0:1, 0:20], taui[:, :], e5sb[:, :],
                                   start=True, stop=True), 1)
            te.wait_ge(c.sem["s_5tm"], tm0 + 2)
            te.wait_ge(c.sem["s_5sf"], sf0 + 2)
            inc("s_5pv", te.matmul(plg5[:, 32:52], onesr5[:, :], taur[:, :],
                                   start=True, stop=True), 1)
            for g in range(NRG):
                te.wait_ge(c.sem["s_5ld"], ld0 + 48 * (g + 1))
                if g >= 1:
                    te.wait_ge(c.sem["s_5ob"], ob0 + 8 * g)
                for kc in range(NC):
                    if kc >= 2:
                        te.wait_ge(c.sem["s_5fe"], fe0 + 8 * g + kc - 1)
                    for q in range(4):
                        te.wait_ge(c.sem["s_5bd"], bd0 + 32 * g + 4 * kc + q + 1)
                        for t31 in range(8 * q, min(8 * q + 8, 31)):
                            for dh in range(2):
                                ins = te.matmul(
                                    prout[:, kc % 2, dh * 512:(dh + 1) * 512],
                                    dgsb[:, q % 2, t31 % 8, :],
                                    vg[:, kc, t31 + dh * 512:t31 + dh * 512 + 512],
                                    start=(t31 == 0), stop=(t31 == 30))
                        inc("s_5fa", ins, 1)
                if g >= 1:
                    te.wait_ge(c.sem["s_5lg"], lg0 + 18 * g)
                for kc in range(NC):
                    for dh in range(2):
                        ins = te.matmul(plg5[0:4, dh * 512:(dh + 1) * 512],
                                        idwsb[:, kc, :], xng[:, kc, dh * 512:(dh + 1) * 512],
                                        start=(kc == 0), stop=(kc == NC - 1))
                inc("s_5pv", ins, 1)
                te.wait_ge(c.sem["s_5tm"], tm0 + 3 + 2 * g + 1)
                for j in range(8):
                    ins = te.transpose(plg5[:, 32 * j:32 * j + 4],
                                       lg5s[0:4, 128 * j:128 * (j + 1)],
                                       c.identf[0:4, 0:4])
                inc("s_5pv", ins, 1)
                for oc in range(16):
                    gb = 16 * g + oc
                    if oc == 0:
                        te.wait_ge(c.sem["s_5fc"], fc0 + 24 * (g + 1))
                        te.wait_ge(c.sem["s_5fe"], fe0 + 8 * (g + 1))
                        te.wait_ge(c.sem["s_5tm"], tm0 + 3 + 2 * g + 2)
                    if oc >= 2:
                        te.wait_ge(c.sem["s_5hm"], hm0 + 16 * g + oc - 1)
                    for kc in range(48):
                        if kc == 0:
                            te.wait_ge(c.sem["s_5w1"], w10 + 16 * (gb + 1))
                        if kc == 24:
                            te.wait_ge(c.sem["s_5w2"], w20 + 16 * (gb + 1))
                        for dh in range(2):
                            ins = te.matmul(prout[:, oc % 2, dh * 512:(dh + 1) * 512],
                                            w1sb[:, kc // 24, kc % 24, :],
                                            rin(kc, slice(dh * 512, (dh + 1) * 512)),
                                            start=(kc == 0), stop=(kc == 47))
                        if kc == 23:
                            inc("s_5r", ins, 1)
                    inc("s_5r", ins, 1)
                    te.wait_ge(c.sem["s_5hm"], hm0 + 16 * g + oc + 1)
                    for dh in range(2):
                        ins = te.matmul(plg5[0:20, dh * 512:(dh + 1) * 512],
                                        w2sb[:, oc, :], hmidb[:, dh * 512:(dh + 1) * 512],
                                        start=(oc == 0), stop=(oc == 15))
                    inc("s_5r", ins, 1)
                te.wait_ge(c.sem["s_5lg"], lg0 + 18 * g + 1)
                for j in range(8):
                    ins = te.transpose(plg5[:, 32 * j:32 * j + 20],
                                       lg5s[0:20, 128 * j:128 * (j + 1)],
                                       c.identf[0:20, 0:20])
                inc("s_5pv", ins, 1)

                for jj in range(9):
                    if jj < 8:
                        for kc in range(NC):
                            key = ("c", g, jj, kc)
                            slot_free_wait(te, key)
                            for i in range(6):
                                src = (fir4[:, i, kc, 128 * jj:128 * (jj + 1)]
                                       if i < 4 else
                                       dg[:, kc, 128 * jj:128 * (jj + 1)] if i == 4
                                       else vg[:, kc, 30 + 128 * jj:30 + 128 * (jj + 1)])
                                ins = te.transpose(ptp[:, slot5[key], i, :], src,
                                                   c.identb[:, :])
                            inc("s_5cb", ins, 1)
                    if jj >= 1:
                        j = jj - 1
                        key = ("o", g, j)
                        slot_free_wait(te, key)
                        te.wait_ge(c.sem["s_5on"], on0 + 1 + 8 * g + j + 1)
                        for kc in range(NC):
                            ins = te.transpose(ptp[:, slot5[key], kc, :], onf[:, kc, :],
                                               c.identb[:, :])
                        inc("s_5fa", ins, 1)
                        te.wait_ge(c.sem["s_5oe"], oe0 + 8 * g + j + 1)
                        if j >= 2:
                            te.wait_ge(c.sem["s_5ob"], ob0 + 8 * g + j - 1)
                        if j < 2:
                            te.wait_ge(c.sem["s_5hm"], hm0 + 16 * (g + 1))
                        gt = 8 * g + j
                        for kc in range(NC):
                            for dh in range(2):
                                te.matmul(prout[:, j % 2, dh * 512:(dh + 1) * 512],
                                          onf[:, kc, :],
                                          wosb[:, kc, dh * 512:(dh + 1) * 512],
                                          start=(kc == 0), stop=False)
                        for dh in range(2):
                            ins = te.matmul(prout[:, j % 2, dh * 512:(dh + 1) * 512],
                                            c.gtfm[:, gt * 128:(gt + 1) * 128],
                                            c.fb2sb[:, dh * 512:(dh + 1) * 512],
                                            start=False, stop=True)
                        inc("s_5r", ins, 1)

        @blk.scalar
        def _(sc):
            sc = W(sc)
            sc.wait_ge(c.sem["s_5wc"], wc_all)
            sc.wait_ge(c.sem["s_5gw"], gw_all)
            inc("s_5tm", sc.activation(taue[:, :], ltausb[:, :], AF.Exp), 1)
            sc.wait_ge(c.sem["s_5pv"], pv0 + 1)
            inc("s_5tm", sc.activation(taur[:, :], plg5[0:1, 0:20], AF.Copy), 1)
            sc.wait_ge(c.sem["s_5pv"], pv0 + 2)
            inc("s_5tm", sc.activation(tauix[:, :], plg5[:, 32:52], AF.Copy), 1)
            for g in range(NRG):
                for kc in range(NC):
                    sc.wait_ge(c.sem["s_5fa"], fa0 + 40 * g + 4 * kc + 4)
                    inc("s_5fe", sc.activation(fir4[:, 3, kc, :],
                                               prout[:, kc % 2, :], AF.Copy), 1)
                sc.wait_ge(c.sem["s_5pv"], pv0 + 3 * g + 3)
                inc("s_5tm", sc.activation(lg5s[0:4, :], plg5[0:4, :], AF.Sigmoid,
                                           bias=idbsb[:, 0:1]), 1)
                sc.wait_ge(c.sem["s_5pv"], pv0 + 3 * g + 4)
                inc("s_5tm", sc.activation(
                    pvtm[:, :, :],
                    bass.AP(plg5, 0, [[RG, 128], [32, 8], [1, 4]]), AF.Copy), 1)
                for oc in range(16):
                    sc.wait_ge(c.sem["s_5r"], r0 + 56 * g + 3 * oc + 2)
                    inc("s_5hm", sc.activation(hmidb[:, :], prout[:, oc % 2, :],
                                               AF.Gelu, bias=c.b1sb[:, oc:oc + 1]), 1)
                sc.wait_ge(c.sem["s_5pv"], pv0 + 3 * g + 5)
                inc("s_5lg", sc.activation(
                    lgtm5[:, :, :],
                    bass.AP(plg5, 0, [[RG, 128], [32, 8], [1, 20]]), AF.Copy), 1)
                for jj in range(9):
                    if jj < 8:
                        sc.wait_ge(c.sem["s_5sf"], sf0 + 2 + 16 * g + 2 * jj + 1)
                        for h in range(H):
                            sc.activation(sfe[:, 5 * h:5 * h + 5],
                                          sft[:, 5 * h:5 * h + 5],
                                          AF.Exp, bias=nm4[:, h:h + 1])
                        inc("s_5lg", sc.nop(), 1)
                        sc.wait_ge(c.sem["s_5sf"], sf0 + 2 + 16 * g + 2 * jj + 2)
                        inc("s_5lg", sc.activation(rms[:, :], rq[:, :], AF.Sqrt), 1)
                    if jj >= 1:
                        j = jj - 1
                        gt = 8 * g + j
                        sc.wait_ge(c.sem["s_5fa"], fa0 + 40 * g + 32 + j + 1)
                        inc("s_5oe", sc.activation(onf[:, :, :],
                                                   ptp[:, slot5[("o", g, j)], :, :],
                                                   AF.Copy), 1)
                        sc.wait_ge(c.sem["s_5ob"], ob0 + gt + 1)
                        inc("s_5os", sc.dma_start(
                            bass.AP(c.out_h, gt * 128 * D, [[D, 128], [1, D]]),
                            outsb[:, :]), 16)

        @blk.vector
        def _(v):
            v = W(v)
            v.wait_ge(c.sem["s_5wc"], wc_all)
            v.wait_ge(c.sem["s_5gw"], gw_all)
            for kc in range(NC):
                v.tensor_scalar(cfb[:, :], wosb[:, kc, :], onwc[:, kc % 2:kc % 2 + 1],
                                None, ALU.mult)
                v.tensor_copy(wosb[:, kc, :], cfb[:, :])
            inc("s_5on", v.nop(), 1)
            v.wait_ge(c.sem["s_5tm"], tm0 + 1)
            v.tensor_scalar(taum[:, :], taue[:, :], 0.5, None, ALU.max)
            v.reciprocal(taui[:, :], taum[:, :])
            inc("s_5sf", v.nop(), 1)
            v.memset(onesr5[:, :], 1.0)
            inc("s_5sf", v.nop(), 1)
            for g in range(NRG):
                v.wait_ge(c.sem["s_5ld"], ld0 + 48 * (g + 1))
                if g >= 1:
                    v.wait_ge(c.sem["s_5hm"], hm0 + 16 * g)
                    v.wait_ge(c.sem["s_5cb"], cb0 + 64 * g)
                for kc in range(NC):
                    for q in range(4):
                        qg = 4 * kc + q
                        if qg >= 2:
                            v.wait_ge(c.sem["s_5fa"], fa0 + 40 * g + qg - 1)
                        elif g >= 1:
                            v.wait_ge(c.sem["s_5fa"], fa0 + 40 * g - 8)
                        for t31 in range(8 * q, min(8 * q + 8, 31)):
                            v.tensor_scalar(dgsb[:, q % 2, t31 % 8, :], c.identb[:, :],
                                            firsb[:, kc, 25 + t31:26 + t31], None,
                                            ALU.mult)
                        inc("s_5bd", v.nop(), 1)
                    for K in (3, 7, 15):
                        p = {3: 0, 7: 1, 15: 2}[K]
                        cb_ = FIRC[K]
                        for jt in range(K):
                            src = vg[:, kc, 30 - (K - 1) + jt:30 - (K - 1) + jt + RG]
                            wcol = firsb[:, kc, cb_ + jt:cb_ + jt + 1]
                            if jt == 0:
                                v.tensor_scalar(cfa[:, :], src, wcol, None, ALU.mult)
                            else:
                                dst = (fir4[:, p, kc, :] if jt == K - 1 else
                                       (cfb[:, :] if jt % 2 == 1 else cfa[:, :]))
                                prev = cfa[:, :] if jt % 2 == 1 else cfb[:, :]
                                v.scalar_tensor_tensor(dst, src, wcol, prev,
                                                       ALU.mult, ALU.add)
                        inc("s_5fc", v.nop(), 1)
                v.wait_ge(c.sem["s_5tm"], tm0 + 3 + 2 * g + 2)
                v.tensor_scalar(tfl[:, :, :], pvtm[:, :, :], 0.09, 0.9, ALU.mult, ALU.add)
                v.tensor_tensor(pvadj[:, :, :], tfl[:, :, :], pvtm[:, :, :], ALU.mult)
                v.tensor_scalar(cmtm[:, :, :], pvadj[:, :, :], -1.0, 1.0, ALU.mult, ALU.add)
                v.wait_ge(c.sem["s_5r"], r0 + 56 * g + 48)
                inc("s_5lg", v.tensor_scalar(lg5s[0:20, :], plg5[0:20, :],
                                             b2sb[:, 0:1], None, ALU.add), 1)
                v.wait_ge(c.sem["s_5tm"], tm0 + 3)
                for jj in range(9):
                  if jj < 8:
                    j = jj
                    gt = 8 * g + j
                    v.wait_ge(c.sem["s_5lg"], lg0 + 18 * g + 2)
                    v.tensor_tensor(sft[:, :], lgtm5[:, j, :], tauix[:, :], ALU.mult)
                    for h in range(H):
                        v.tensor_reduce(nm4[:, h:h + 1], sft[:, 5 * h:5 * h + 5],
                                        AX.X, ALU.max)
                    v.drain()
                    v.tensor_scalar(nm4[:, :], nm4[:, :], -1.0, None, ALU.mult)
                    inc("s_5sf", v.nop(), 1)
                    v.wait_ge(c.sem["s_5lg"], lg0 + 18 * g + 3 + 2 * j)
                    for h in range(H):
                        v.tensor_reduce(ssum[:, h:h + 1], sfe[:, 5 * h:5 * h + 5],
                                        AX.X, ALU.add)
                    v.drain()
                    v.reciprocal(sri[:, :], ssum[:, :])
                    for h in range(H):
                        v.tensor_scalar(ptm5[:, j, 5 * h:5 * h + 5],
                                        sfe[:, 5 * h:5 * h + 5], sri[:, h:h + 1],
                                        None, ALU.mult)
                    for h in range(H):
                        v.tensor_scalar(cmp5[:, 5 * h:5 * h + 5],
                                        ptm5[:, j, 5 * h:5 * h + 5],
                                        cmtm[:, j, h:h + 1], None, ALU.mult)
                    for kc in range(NC):
                        h = kc // 2
                        key = ("c", g, j, kc)
                        v.wait_ge(c.sem["s_5cb"], cb0 + cidx5[key])
                        sl = slot5[key]
                        v.tensor_scalar(cfb[:, 0:128], ptp[:, sl, 5, :],
                                        pvadj[:, j, h:h + 1], None, ALU.mult)
                        for r in range(5):
                            dst = (otF[:, kc, :] if r == 4 else
                                   cfb[:, 128:256] if r % 2 == 0 else cfb[:, 0:128])
                            prev = cfb[:, 0:128] if r % 2 == 0 else cfb[:, 128:256]
                            ins = v.scalar_tensor_tensor(
                                dst, ptp[:, sl, r, :],
                                cmp5[:, 5 * h + r:5 * h + r + 1],
                                prev, ALU.mult, ALU.add)
                        inc("s_5cv", ins, 1)
                    cfa3 = bass.AP(cfa, 0, [[RG, 128], [128, 8], [1, 128]])
                    v.tensor_tensor(cfa3, otF[:, :, :], otF[:, :, :], ALU.mult)
                    for h in range(H):
                        v.tensor_reduce(sumq[:, h:h + 1], cfa[:, 256 * h:256 * (h + 1)],
                                        AX.X, ALU.add)
                    v.drain()
                    v.tensor_scalar(sqb[:, :], sumq[:, :], 1.0 / 256, 1e-5,
                                    ALU.mult, ALU.add)
                    v.reciprocal(rq[:, :], sqb[:, :])
                    inc("s_5sf", v.nop(), 1)
                    v.wait_ge(c.sem["s_5lg"], lg0 + 18 * g + 4 + 2 * j)
                    if gt >= 1:
                        v.wait_ge(c.sem["s_5oe"], oe0 + gt)
                        gp, jp = (gt - 1) // 8, (gt - 1) % 8
                        v.wait_ge(c.sem["s_5r"], r0 + 56 * gp + 48 + jp + 1)
                    for kc in range(NC):
                        h = kc // 2
                        v.tensor_scalar(onf[:, kc, :], otF[:, kc, :], rms[:, h:h + 1],
                                        None, ALU.mult)
                    inc("s_5on", v.nop(), 1)
                  if jj >= 1:
                    j = jj - 1
                    gt = 8 * g + j
                    v.wait_ge(c.sem["s_5r"], r0 + 56 * g + 48 + j + 1)
                    v.wait_ge(c.sem["s_5xr"], xr0 + 16 * (gt + 1))
                    if gt >= 1:
                        v.wait_ge(c.sem["s_5os"], os0 + 16 * gt)
                    Q5 = gt // 2
                    v.wait_ge(c.sem["s_5gt" if Q5 % 2 == 0 else "s_5g2"],
                              (gt0 if Q5 % 2 == 0 else g20) + 16 * (Q5 // 2 + 1))
                    v.tensor_tensor(cfa[:, :], prout[:, j % 2, :], xres[:, :], ALU.add)
                    v.drain()
                    v.scalar_tensor_tensor(cfb[:, :], rowsb[:, (gt // 2) % 2, gt % 2, :],
                                           c.geff0[:, gt:gt + 1], cfa[:, :],
                                           ALU.mult, ALU.add)
                    v.drain()
                    v.scalar_tensor_tensor(outsb[:, :], rowsb[:, (gt // 2) % 2, 2 + gt % 2, :],
                                           c.geff1[:, gt:gt + 1], cfb[:, :],
                                           ALU.mult, ALU.add)
                    inc("s_5ob", v.nop(), 1)


def _fin(c):
    nc, inc, n_of = c.nc, c.inc, c.n_of
    with nc.Block() as blk:
        c.barrier(blk)

        @blk.sync
        def _(sy):
            sy.wait_ge(c.sem["s_5os"], n_of("s_5os"))
            sy.wait_ge(c.sem["s_fin"], n_of("s_fin"))


def _host_consts():
    idf = np.eye(128, dtype=np.float32)
    idb16 = np.eye(128, dtype=np.float32).astype(ml_dtypes.bfloat16)
    iota8 = np.tile(np.arange(8, dtype=np.float32), (128, NT))
    iotap = iota8 + 1e4
    eye32 = np.zeros((32, 128), np.float32)
    slow32 = np.zeros((32, 128), np.float32)
    low32 = np.zeros((32, 128), np.float32)
    for b in range(4):
        eye32[:, b * 32:(b + 1) * 32] = np.eye(32)
        low32[:, b * 32:(b + 1) * 32] = np.tril(np.ones((32, 32)))
        slow32[:, b * 32:(b + 1) * 32] = np.tril(np.ones((32, 32)), -1)
    onesb = np.ones((8, 2), np.float32).astype(ml_dtypes.bfloat16)
    ones1r = np.ones((1, 128), np.float32).astype(ml_dtypes.bfloat16)
    colsel = np.zeros((128, 16), np.float32)
    for h in range(4):
        colsel[:, 4 * h + h] = 1.0
    colsel = colsel.astype(ml_dtypes.bfloat16)
    caplv = (np.arange(8, dtype=np.float32) * CAPL - 1.0)[:, None]
    ones128 = np.ones((128, 1), np.float32).astype(ml_dtypes.bfloat16)
    bcast4 = np.zeros((4, 512), np.float32)
    for h in range(4):
        bcast4[h, 128 * h:128 * (h + 1)] = 1.0
    bcast4 = bcast4.astype(ml_dtypes.bfloat16)
    ii, jj = np.arange(128)[:, None] % 32, np.arange(128)[None, :] % 32
    slow128 = (jj < ii).astype(np.float32)
    eyeb128 = (jj == ii).astype(np.float32)
    lowi128 = (jj <= ii).astype(np.float32)
    e5 = np.zeros((4, 20), np.float32)
    for h in range(4):
        e5[h, 5 * h:5 * h + 5] = 1.0
    tokid = np.zeros((128, NT, 128), np.int16)
    tid = (np.arange(NT)[None, :] * 128 + np.arange(128)[:, None]).astype(np.int16)
    tokid[:, :, :] = tid[:, :, None]
    return dict(idf=idf, idb16=idb16, iota8c=iota8, iotapc=iotap,
                eye32c=eye32, eye32x2c=2.0 * eye32, slow32c=slow32, low32c=low32,
                onesbc=onesb, ones1rc=ones1r, colselc=colsel, caplvc=caplv,
                tokidc=tokid, ones128c=ones128, bcast4c=bcast4, slow128c=slow128,
                eyeb128c=eyeb128, eyeb2c=2.0 * eyeb128, lowi128c=lowi128, e5c=e5)


def _host_inputs(core, inputs):
    p = "L0_attn_"
    f = "L0_ffn_"
    consts = _host_consts()
    premask = np.zeros((8, 2), np.float32)
    premask[:core, 0] = 1.0
    premask[:, 1] = 1.0
    d = dict(
        x=np.ascontiguousarray(inputs[p + "hidden_states"][core]),
        wq=inputs[p + "Wq"], wk=inputs[p + "Wk"], wv=inputs[p + "Wv"],
        qcw=np.ascontiguousarray(inputs[p + "q_conv_w"].reshape(NC, 128, 4).transpose(1, 0, 2)),
        kcw=np.ascontiguousarray(inputs[p + "k_conv_w"].reshape(NC, 128, 4).transpose(1, 0, 2)),
        vcw=np.ascontiguousarray(inputs[p + "v_conv_w"].reshape(NC, 128, 4).transpose(1, 0, 2)),
        wb=inputs[p + "Wb"],
        fir=np.ascontiguousarray(np.concatenate(
            [inputs[p + "fir_w3"].reshape(D, 3), inputs[p + "fir_w7"].reshape(D, 7),
             inputs[p + "fir_w15"].reshape(D, 15), inputs[p + "fir_w31"].reshape(D, 31)],
            axis=1).reshape(NC, 128, 56).transpose(1, 0, 2)),
        w1=inputs[p + "router_W1"],
        b1=np.ascontiguousarray(inputs[p + "router_b1"].reshape(16, 128).T),
        w2=inputs[p + "router_W2"],
        b2=inputs[p + "router_b2"][:, None],
        idw=inputs[p + "id_gate_W"],
        idb=inputs[p + "id_gate_b"][:, None],
        ltau=inputs[p + "log_tau"][:, None],
        onw=np.ascontiguousarray(np.tile(inputs[p + "o_norm_w"], 4).reshape(8, 128).T[:, 0:2]),
        wo=inputs[p + "Wo"],
        gate=np.ascontiguousarray(inputs[f + "gate_w"].T),
        fc1t=np.ascontiguousarray(
            inputs[f + "fc1_w"].reshape(E, 32, 128, D).transpose(0, 1, 3, 2)),
        fb1=np.ascontiguousarray(
            inputs[f + "fc1_b"].reshape(E, 32, 128).transpose(2, 0, 1).reshape(128, E * 32)),
        fc2=inputs[f + "fc2_w"],
        fb2=np.ascontiguousarray(inputs[f + "fc2_b"].reshape(E, D)),
        premaskc=premask,
        **consts,
    )
    return {k: np.ascontiguousarray(v) for k, v in d.items()}


_cache = {}


def kernel(**inputs):
    key = "full"
    if key not in _cache:
        nc = build()
        lower_extended_insts(nc)
        _cache[key] = nc
    nc = _cache[key]
    in_maps = [_host_inputs(core, inputs) for core in range(8)]
    res = run_bass_kernel_spmd(nc, in_maps, core_ids=list(range(8)))
    out = np.stack([res.results[cb]["out"] for cb in range(8)], axis=0)
    return out.astype(np.float32)
